# version 08
# baseline (speedup 1.0000x reference)
import os
import contextlib
import numpy as np
import concourse.bass as bass
import concourse.mybir as mybir
from concourse.bass_utils import run_bass_kernel_spmd
from concourse.alu_op_type import AluOpType

F32 = mybir.dt.float32
BF16 = mybir.dt.bfloat16
AF = mybir.ActivationFunctionType
AX = mybir.AxisListType
OP = AluOpType

T = 4096
D = 1024
H = 16
FF = 4096
NL = 4
NE = 4
TCH = 1024
NCH = T // TCH
EPS = 1e-5


class Sem:
    def __init__(self, h):
        self.h = h
        self.n = 0


class Bld:
    ENGS = ("sync", "scalar", "vector", "tensor", "gpsimd")

    def __init__(self):
        self.streams = {e: [] for e in self.ENGS}

    def emit(self, eng, fn):
        self.streams[eng].append(fn)

    def wait(self, eng, sem, tgt):
        if tgt > 0:
            self.emit(eng, lambda E, s=sem.h, t=tgt: E.wait_ge(s, t))

    def barrier(self, sems):
        for eng in self.ENGS:
            for s in sems:
                self.wait(eng, s, s.n)

    def run(self, block):
        for eng in self.ENGS:
            fns = self.streams[eng]
            if not fns:
                continue

            def body(E, fns=fns):
                for f in fns:
                    f(E)

            getattr(block, eng)(body)
        self.streams = {e: [] for e in self.ENGS}


def build(nc, phases=4):
    dt = nc.dram_tensor
    x_in = dt("x_in", [T, D], F32, kind="ExternalInput")
    idn = dt("idn", [128, 128], F32, kind="ExternalInput")
    wqd = dt("wqd", [128, 4096], F32, kind="ExternalInput")
    wqk = dt("wqk", [128, 4096], F32, kind="ExternalInput")
    wkv = dt("wkv", [128, 512], F32, kind="ExternalInput")
    wvu = dt("wvu", [64, 1024], F32, kind="ExternalInput")
    wpj = dt("wpj", [128, 8192], F32, kind="ExternalInput")
    bqd = dt("bqd", [128, 4], F32, kind="ExternalInput")
    bqk = dt("bqk", [64, 16], F32, kind="ExternalInput")
    bkv = dt("bkv", [64, 1], F32, kind="ExternalInput")
    bvu = dt("bvu", [128, 8], F32, kind="ExternalInput")
    bpj = dt("bpj", [128, 8], F32, kind="ExternalInput")
    w1l = dt("w1l", [NL, NE, 16, 128, 2048], F32, kind="ExternalInput")
    w2l = dt("w2l", [NL, NE, 16, 128, 2048], F32, kind="ExternalInput")
    b1c = dt("b1c", [128, NL * NE * 32], F32, kind="ExternalInput")
    b2c = dt("b2c", [128, NL * NE * 8], F32, kind="ExternalInput")
    rwl = dt("rwl", [128, NL * 8 * 4], F32, kind="ExternalInput")
    rbr = dt("rbr", [1, NL * 4], F32, kind="ExternalInput")
    gcl = dt("gcl", [128, NL * NE * 8], F32, kind="ExternalInput")
    oh4 = dt("oh4", [4, 640], F32, kind="ExternalInput")
    bcl = dt("bcl", [128, NL * NE * 8], F32, kind="ExternalInput")
    out = dt("out", [T, D], F32, kind="ExternalOutput")
    _dbg = "ExternalOutput" if os.environ.get("K_DBG") else "Internal"
    xn_s = dt("xn_s", [128, 8 * T], F32, kind=_dbg)
    s_s = dt("s_s", [128, 8 * T], F32, kind=_dbg)
    m_s = dt("m_s", [16, T], F32, kind=_dbg)
    v_s = dt("v_s", [1, T], F32, kind=_dbg)

    es0 = contextlib.ExitStack()
    with es0:
        mksem = lambda n: Sem(es0.enter_context(nc.semaphore(n)))
        s_gd = mksem("s_gd")
        s_sd = mksem("s_sd")
        s_t = mksem("s_t")
        s_sc = mksem("s_sc")
        s_v = mksem("s_v")
        b = Bld()

        idn_s = es0.enter_context(nc.sbuf_tensor("idn_s", [128, 128], F32))
        ones1b = es0.enter_context(nc.sbuf_tensor("ones1b", [1, 128], BF16))
        ones1f = es0.enter_context(nc.sbuf_tensor("ones1f", [1, 128], F32))
        ones64b = es0.enter_context(nc.sbuf_tensor("ones64b", [64, 1], BF16))
        ones128b = es0.enter_context(nc.sbuf_tensor("ones128b", [128, 1], BF16))
        epsc = es0.enter_context(nc.sbuf_tensor("epsc", [128, 1], F32))

        def dma(eng, dst, src, sem=None):
            sem = sem or (s_gd if eng == "gpsimd" else s_sd)

            def f(E, dst=dst, src=src, sh=sem.h):
                E.dma_start(dst, src).then_inc(sh, 16)

            b.emit(eng, f)
            sem.n += 16
            return sem.n

        idn_done = dma("sync", idn_s[:], idn.ap())

        def init_memsets(E):
            E.memset(ones1b[:], 1.0)
            E.memset(ones1f[:], 1.0)
            E.memset(ones64b[:], 1.0)
            E.memset(ones128b[:], 1.0)
            E.memset(epsc[:], EPS).then_inc(s_v.h, 1)

        b.emit("vector", init_memsets)
        s_v.n += 1
        init_v = s_v.n

        es_ab = contextlib.ExitStack()
        es_xn = contextlib.ExitStack()

        xn_bf = es_xn.enter_context(nc.sbuf_tensor("xn_bf", [128, 8 * T], BF16, side="left"))
        with contextlib.ExitStack() as es1:
            sb1 = lambda n, s, d: es1.enter_context(nc.sbuf_tensor(n, s, d, side="right"))
            xt = sb1("xt", [128, 2 * 1024], F32)
            sqt = sb1("sqt", [128, 1024], F32)
            xnt = sb1("xnt", [128, 2 * 1024], F32)
            stg = sb1("stg", [128, 2 * 1024], F32)
            mucol = sb1("mucol", [128, 4], F32)
            vstg = sb1("vstg", [128, 32], F32)
            pstr = [es1.enter_context(nc.psum_tensor("pstr0", [128, 512], F32)),
                    es1.enter_context(nc.psum_tensor("pstr1", [128, 512], F32))]

            x_dma = {}
            xn_v = {}
            tr_ev = {}
            cast_sc = {}
            stg_dma = {}
            sq_used = {}
            _p1cut = int(os.environ.get("K_P1CUT", 99))
            for jt in range(int(os.environ.get("K_P1N", T // 128))):
                sl = jt % 2
                if jt >= 2:
                    b.wait("sync", s_v, xn_v[jt - 2])
                x_dma[jt] = dma("sync", xt[:, sl * 1024:(sl + 1) * 1024],
                                x_in.ap()[jt * 128:(jt + 1) * 128, :])
                if _p1cut < 2:
                    continue
                b.wait("scalar", s_sd, x_dma[jt])
                if jt >= 1:
                    b.wait("scalar", s_v, sq_used[jt - 1])

                def p1sq(E, sl=sl):
                    E.activation(sqt[:], xt[:, sl * 1024:(sl + 1) * 1024],
                                 AF.Square).then_inc(s_sc.h, 1)

                b.emit("scalar", p1sq)
                s_sc.n += 1
                if _p1cut < 3:
                    continue
                b.wait("vector", s_sc, s_sc.n)
                b.wait("vector", s_sd, x_dma[jt])
                if jt == 0:
                    b.wait("vector", s_v, init_v)

                def p1stats(E, sl=sl, jt=jt, base=s_v.n):
                    xs = xt[:, sl * 1024:(sl + 1) * 1024]
                    E.reduce_sum(mucol[:, 0:1], xs, axis=AX.X)
                    E.reduce_sum(mucol[:, 1:2], sqt[:], axis=AX.X).then_inc(s_v.h, 1)
                    E.wait_ge(s_v.h, base + 1)
                    E.tensor_scalar(mucol[:, 0:1], mucol[:, 0:1], 1.0 / D, None, OP.mult, OP.bypass)
                    E.tensor_scalar(mucol[:, 1:2], mucol[:, 1:2], 1.0 / D, None, OP.mult, OP.bypass).then_inc(s_v.h, 1)
                    E.wait_ge(s_v.h, base + 2)
                    E.tensor_scalar(mucol[:, 2:3], mucol[:, 0:1], mucol[:, 0:1], None, OP.mult, OP.bypass).then_inc(s_v.h, 1)
                    E.wait_ge(s_v.h, base + 3)
                    E.tensor_tensor(mucol[:, 1:2], mucol[:, 1:2], mucol[:, 2:3], OP.subtract).then_inc(s_v.h, 1)
                    E.wait_ge(s_v.h, base + 4)
                    E.tensor_copy(vstg[:, jt:jt + 1], mucol[:, 1:2]).then_inc(s_v.h, 1)

                b.emit("vector", p1stats)
                s_v.n += 5
                sq_used[jt] = s_v.n
                if _p1cut < 4:
                    continue
                b.wait("scalar", s_v, s_v.n)

                def p1rstd(E, base=s_sc.n):
                    E.activation(mucol[:, 2:3], mucol[:, 1:2], AF.Ln, bias=epsc[:]).then_inc(s_sc.h, 1)
                    E.wait_ge(s_sc.h, base + 1)
                    E.activation(mucol[:, 3:4], mucol[:, 2:3], AF.Exp, scale=-0.5).then_inc(s_sc.h, 1)

                b.emit("scalar", p1rstd)
                s_sc.n += 2
                if _p1cut < 5:
                    continue
                b.wait("vector", s_sc, s_sc.n)
                if jt >= 2:
                    b.wait("vector", s_t, tr_t[jt - 2])

                def p1xn(E, sl=sl):
                    E.tensor_scalar(xnt[:, sl * 1024:(sl + 1) * 1024],
                                    xt[:, sl * 1024:(sl + 1) * 1024],
                                    mucol[:, 0:1], mucol[:, 3:4], OP.subtract, OP.mult).then_inc(s_v.h, 1)

                b.emit("vector", p1xn)
                s_v.n += 1
                xn_v[jt] = s_v.n
                tr_t = tr_t if jt > 0 else {}
                if _p1cut < 6:
                    continue

                b.wait("tensor", s_v, xn_v[jt])
                if jt == 0:
                    b.wait("tensor", s_sd, idn_done)
                for half in range(2):
                    gi = jt * 2 + half
                    if gi >= 2:
                        b.wait("tensor", s_v, tr_ev[gi - 2])

                    def p1tr(E, sl=sl, half=half):
                        last = None
                        for k in range(4):
                            kd = half * 4 + k
                            last = E.transpose(pstr[half][:, k * 128:(k + 1) * 128],
                                               xnt[:, sl * 1024 + kd * 128: sl * 1024 + (kd + 1) * 128],
                                               idn_s[:])
                        last.then_inc(s_t.h, 1)

                    b.emit("tensor", p1tr)
                    s_t.n += 1
                    b.wait("vector", s_t, s_t.n)
                    if gi >= 4:
                        b.wait("vector", s_sc, cast_sc.get(jt - 2, 0))
                        b.wait("vector", s_sd, stg_dma.get(jt - 2, 0))

                    def p1ev(E, sl=sl, half=half):
                        E.tensor_copy(stg[:, sl * 1024 + half * 512: sl * 1024 + (half + 1) * 512],
                                      pstr[half][:]).then_inc(s_v.h, 1)

                    b.emit("vector", p1ev)
                    s_v.n += 1
                    tr_ev[gi] = s_v.n
                tr_t[jt] = s_t.n
                if _p1cut < 7:
                    continue
                b.wait("scalar", s_v, tr_ev[jt * 2 + 1])

                def p1cast(E, sl=sl, jt=jt):
                    E.activation(
                        bass.AP(xn_bf, jt * 128, [[8 * T, 128], [T, 8], [1, 128]]),
                        stg.ap()[:, sl * 1024:(sl + 1) * 1024].rearrange("p (c f) -> p c f", c=8),
                        AF.Copy).then_inc(s_sc.h, 1)

                b.emit("scalar", p1cast)
                s_sc.n += 1
                cast_sc[jt] = s_sc.n
                if _p1cut < 8:
                    continue
                b.wait("sync", s_v, tr_ev[jt * 2 + 1])
                stg_dma[jt] = dma(
                    "sync", bass.AP(xn_s, jt * 128, [[8 * T, 128], [T, 8], [1, 128]]),
                    stg.ap()[:, sl * 1024:(sl + 1) * 1024].rearrange("p (c f) -> p c f", c=8))
            p1_cast_done = s_sc.n
            b.wait("sync", s_v, s_v.n)
            for jt in range(T // 128):
                dma("sync", bass.AP(v_s, jt * 128, [[1, 1], [1, 128]]),
                    vstg[:, jt:jt + 1])
            b.barrier([s_t, s_sc, s_v, s_gd, s_sd])


        def _checkpoint():
            b.wait("gpsimd", s_gd, s_gd.n)
            b.wait("sync", s_sd, s_sd.n)
            b.barrier([s_t, s_sc, s_v])
            blk_ = es0.enter_context(nc.Block())
            b.run(blk_)
        if phases < 2:
            _checkpoint()
            es_xn.close()
            return nc

        sab = lambda n, s, d: es_ab.enter_context(nc.sbuf_tensor(n, s, d, side="right"))
        ckv = sab("ckv", [65, T], BF16)
        wvu_s = sab("wvu_s", [64, 1024], BF16)
        wpj_s = sab("wpj_s", [128, 8192], BF16)
        btot = sab("btot", [128, 8], F32)
        kn2m = sab("kn2m", [1, 1], F32)

        with contextlib.ExitStack() as es2a:
            sb2 = lambda n, s, d: es2a.enter_context(nc.sbuf_tensor(n, s, d, side="right"))
            cq = sb2("cq", [128, 4 * T], BF16)
            wqd_s = sb2("wqd_s", [128, 4096], BF16)
            wkv_s = sb2("wkv_s", [128, 512], BF16)
            wqk_s = sb2("wqk_s", [128, 4096], BF16)
            bqd_s = sb2("bqd_s", [128, 4], F32)
            bkv_s = sb2("bkv_s", [64, 1], F32)
            bqk_s = sb2("bqk_s", [64, 16], F32)
            bvu_s = sb2("bvu_s", [128, 8], F32)
            bvu_b = sb2("bvu_b", [128, 8], BF16)
            bpj_s = sb2("bpj_s", [128, 8], F32)
            mstg = sb2("mstg", [1, 2 * T], F32)
            psq = [es2a.enter_context(nc.psum_tensor("psq0", [128, 512], F32)),
                   es2a.enter_context(nc.psum_tensor("psq1", [128, 512], F32))]
            psn = es2a.enter_context(nc.psum_tensor("psn", [128, 1024], F32))

            wq_done = dma("gpsimd", wqd_s[:], wqd.ap())
            wkv_done = dma("gpsimd", wkv_s[:], wkv.ap())
            wqk_done = dma("gpsimd", wqk_s[:], wqk.ap())
            wvu_done = dma("gpsimd", wvu_s[:], wvu.ap())
            wpj_done = dma("gpsimd", wpj_s[:], wpj.ap())
            dma("sync", bqd_s[:], bqd.ap())
            dma("sync", bkv_s[:], bkv.ap())
            dma("sync", bqk_s[:], bqk.ap())
            dma("sync", bvu_s[:], bvu.ap())
            bias_done = dma("sync", bpj_s[:], bpj.ap())

            b.wait("vector", s_sd, bias_done)

            def p2bias(E):
                E.tensor_scalar(bqk_s[:], bqk_s[:], 0.125, None, OP.mult, OP.bypass)
                E.tensor_copy(bvu_b[:], bvu_s[:])
                E.memset(ckv[64:65, :], -1.0).then_inc(s_v.h, 1)

            b.emit("vector", p2bias)
            s_v.n += 1
            bias_v = s_v.n

            b.wait("tensor", s_gd, wkv_done)
            b.wait("tensor", s_sc, p1_cast_done)
            ckv_sc = {}
            for tq in range(8):
                if tq >= 2:
                    b.wait("tensor", s_sc, ckv_sc[tq - 2])

                def ckvmm(E, tq=tq):
                    last = None
                    for kd in range(8):
                        last = E.matmul(psq[tq % 2][0:64, :],
                                        wkv_s[:, kd * 64:(kd + 1) * 64],
                                        xn_bf[:, kd * T + tq * 512: kd * T + (tq + 1) * 512],
                                        start=(kd == 0), stop=(kd == 7))
                    last.then_inc(s_t.h, 1)

                b.emit("tensor", ckvmm)
                s_t.n += 1
                b.wait("scalar", s_t, s_t.n)
                if tq == 0:
                    b.wait("scalar", s_sd, bias_done)

                def ckvev(E, tq=tq):
                    E.activation(ckv[0:64, tq * 512:(tq + 1) * 512], psq[tq % 2][0:64, :],
                                 AF.Identity, bias=bkv_s[:]).then_inc(s_sc.h, 1)

                b.emit("scalar", ckvev)
                s_sc.n += 1
                ckv_sc[tq] = s_sc.n
            ckv_done = s_sc.n

            b.wait("tensor", s_gd, wq_done)
            cq_sc = {}
            for g in range(32):
                mq, tq = g // 8, g % 8
                if g >= 2:
                    b.wait("tensor", s_sc, cq_sc[g - 2])
                elif g == 0:
                    b.wait("tensor", s_sc, ckv_done)

                def cqmm(E, mq=mq, tq=tq, g=g):
                    last = None
                    for kd in range(8):
                        last = E.matmul(psq[g % 2][:],
                                        wqd_s[:, (kd * 4 + mq) * 128:(kd * 4 + mq + 1) * 128],
                                        xn_bf[:, kd * T + tq * 512: kd * T + (tq + 1) * 512],
                                        start=(kd == 0), stop=(kd == 7))
                    last.then_inc(s_t.h, 1)

                b.emit("tensor", cqmm)
                s_t.n += 1
                b.wait("scalar", s_t, s_t.n)

                def cqev(E, mq=mq, tq=tq, g=g):
                    E.activation(cq[:, mq * T + tq * 512: mq * T + (tq + 1) * 512],
                                 psq[g % 2][:], AF.Identity,
                                 bias=bqd_s[:, mq:mq + 1]).then_inc(s_sc.h, 1)

                b.emit("scalar", cqev)
                s_sc.n += 1
                cq_sc[g] = s_sc.n
            cq_done = s_sc.n

            es_xn.close()
            q_all = es_ab.enter_context(nc.sbuf_tensor("q_all", [128, 8 * T], BF16, side="left"))
            ckvsq = sb2("ckvsq", [64, T], BF16)
            kn2r = sb2("kn2r", [1, T], F32)
            sqst = sb2("sqst", [64, 2 * 512], BF16)

            b.wait("scalar", s_sc, ckv_done)

            def ksq(E):
                E.activation(ckvsq[:], ckv[0:64, :], AF.Square).then_inc(s_sc.h, 1)

            b.emit("scalar", ksq)
            s_sc.n += 1
            b.wait("tensor", s_sc, s_sc.n)
            b.wait("tensor", s_v, init_v)
            kn_v = {}
            for tq in range(8):
                if tq > 0:
                    b.wait("tensor", s_v, kn_v[tq - 1])

                def knmm(E, tq=tq):
                    E.matmul(psn[0:1, 0:512], ones64b[:],
                             ckvsq[:, tq * 512:(tq + 1) * 512],
                             start=True, stop=True).then_inc(s_t.h, 1)

                b.emit("tensor", knmm)
                s_t.n += 1
                b.wait("vector", s_t, s_t.n)

                def knev(E, tq=tq):
                    E.tensor_copy(kn2r[:, tq * 512:(tq + 1) * 512],
                                  psn[0:1, 0:512]).then_inc(s_v.h, 1)

                b.emit("vector", knev)
                s_v.n += 1
                kn_v[tq] = s_v.n

            def knmax(E):
                E.reduce_max(kn2m[:], kn2r[:], axis=AX.X).then_inc(s_v.h, 1)

            b.emit("vector", knmax)
            s_v.n += 1
            kn2m_v = s_v.n

            b.wait("tensor", s_gd, wqk_done)
            qg_sc = {}
            for g in range(H * 8):
                h, tq = g // 8, g % 8
                r0, ch = 64 * (h % 2), h // 2
                if g >= 2:
                    b.wait("tensor", s_sc, qg_sc[g - 2])
                elif g == 0:
                    b.wait("tensor", s_sc, cq_done)

                def qmm(E, h=h, tq=tq, g=g):
                    last = None
                    for kc in range(4):
                        last = E.matmul(psq[g % 2][0:64, :],
                                        wqk_s[:, (kc * 16 + h) * 64:(kc * 16 + h + 1) * 64],
                                        cq[:, kc * T + tq * 512: kc * T + (tq + 1) * 512],
                                        start=(kc == 0), stop=(kc == 3))
                    last.then_inc(s_t.h, 1)

                b.emit("tensor", qmm)
                s_t.n += 1
                b.wait("scalar", s_t, s_t.n)
                if g == 0:
                    b.wait("scalar", s_v, bias_v)

                def qev(E, h=h, tq=tq, g=g, r0=r0, ch=ch):
                    E.activation(q_all[r0:r0 + 64, ch * T + tq * 512: ch * T + (tq + 1) * 512],
                                 psq[g % 2][0:64, :], AF.Identity, scale=0.125,
                                 bias=bqk_s[:, h:h + 1]).then_inc(s_sc.h, 1)

                b.emit("scalar", qev)
                s_sc.n += 1
                qg_sc[g] = s_sc.n

            qn_t = {}
            m_sc = {}
            m_dma = {}
            for g in range(H * 8):
                h, tq = g // 8, g % 8
                r0, ch = 64 * (h % 2), h // 2
                if g >= 2:
                    b.wait("scalar", s_t, qn_t[g - 2])
                if g == 0:
                    b.wait("scalar", s_sc, qg_sc[H * 8 - 1])

                def qsq(E, g=g, r0=r0, ch=ch, tq=tq):
                    E.activation(sqst[:, (g % 2) * 512:(g % 2 + 1) * 512],
                                 q_all[r0:r0 + 64, ch * T + tq * 512: ch * T + (tq + 1) * 512],
                                 AF.Square).then_inc(s_sc.h, 1)

                b.emit("scalar", qsq)
                s_sc.n += 1
                b.wait("tensor", s_sc, s_sc.n)
                if g == 0:
                    b.wait("tensor", s_v, kn2m_v)
                if g > 0:
                    b.wait("tensor", s_sc, m_sc[g - 1])

                def qnmm(E, g=g):
                    E.matmul(psn[0:1, 512:1024], ones64b[:],
                             sqst[:, (g % 2) * 512:(g % 2 + 1) * 512],
                             start=True, stop=True).then_inc(s_t.h, 1)

                b.emit("tensor", qnmm)
                s_t.n += 1
                qn_t[g] = s_t.n
                b.wait("scalar", s_t, s_t.n)
                if g >= 16:
                    b.wait("scalar", s_sd, m_dma.get(h - 1, 0))

                def mev(E, h=h, tq=tq):
                    E.activation(mstg[:, (h % 2) * T + tq * 512:(h % 2) * T + (tq + 1) * 512],
                                 psn[0:1, 512:1024], AF.Sqrt, scale=kn2m[:]).then_inc(s_sc.h, 1)

                b.emit("scalar", mev)
                s_sc.n += 1
                m_sc[g] = s_sc.n
                if tq == 7:
                    b.wait("sync", s_sc, s_sc.n)
                    m_dma[h] = dma("sync", m_s.ap()[h:h + 1, :],
                                   mstg[:, (h % 2) * T:(h % 2 + 1) * T])

            b.wait("tensor", s_gd, wpj_done)
            b.wait("tensor", s_v, bias_v)
            b.wait("tensor", s_sc, m_sc[H * 8 - 1])

            def btmm(E):
                last = None
                for mc in range(8):
                    for kv in range(8):
                        last = E.matmul(psn[:, mc:mc + 1],
                                        wpj_s[:, (kv * 8 + mc) * 128:(kv * 8 + mc + 1) * 128],
                                        bvu_b[:, kv:kv + 1],
                                        start=(kv == 0), stop=(kv == 7))
                last.then_inc(s_t.h, 1)

            b.emit("tensor", btmm)
            s_t.n += 1
            b.wait("vector", s_t, s_t.n)

            def btev(E, base=s_v.n):
                E.tensor_copy(btot[:], psn[:, 0:8]).then_inc(s_v.h, 1)
                E.wait_ge(s_v.h, base + 1)
                E.tensor_tensor(btot[:], btot[:], bpj_s[:], OP.add).then_inc(s_v.h, 1)

            b.emit("vector", btev)
            s_v.n += 2
            b.barrier([s_t, s_sc, s_v, s_gd, s_sd])

        if phases < 3:
            _checkpoint()
            es_ab.close()
            return nc

        with contextlib.ExitStack() as es2b:
            sb2 = lambda n, s, d: es2b.enter_context(nc.sbuf_tensor(n, s, d, side="right"))
            y_fm = sb2("y_fm", [128, 8 * T], BF16)
            qa = [sb2("qa0", [65, T], BF16), sb2("qa1", [65, T], BF16)]
            vh = [sb2("vh0", [128, 32 * 65], BF16), sb2("vh1", [128, 32 * 65], BF16)]
            pt = [sb2("pt0", [128, 1024], BF16), sb2("pt1", [128, 1024], BF16)]
            rrow = sb2("rrow", [1, 1024], F32)
            rbsb = sb2("rbsb", [64, 1024], F32)
            sstg = sb2("sstg", [128, 2 * 512], F32)
            psS = [es2b.enter_context(nc.psum_tensor("psS0", [128, 1024], F32)),
                   es2b.enter_context(nc.psum_tensor("psS1", [128, 1024], F32))]
            psY = [es2b.enter_context(nc.psum_tensor("psY0", [65, 1024], F32)),
                   es2b.enter_context(nc.psum_tensor("psY1", [65, 1024], F32))]

            def vones(E):
                E.memset(bass.AP(vh[0], 64, [[32 * 65, 128], [65, 32]]), 1.0)
                E.memset(bass.AP(vh[1], 64, [[32 * 65, 128], [65, 32]]), 1.0).then_inc(s_v.h, 1)

            b.emit("vector", vones)
            s_v.n += 1

            qa_q_v = {}
            qa_m_d = {}
            head_t = {}
            vh_v = {}
            last_ytt = {}
            exp_last = {}
            rb_ev = 0

            for h in range(H):
                r0, ch, pp = 64 * (h % 2), h // 2, h % 2
                if h >= 2:
                    b.wait("vector", s_t, head_t[h - 2])
                    b.wait("gpsimd", s_t, head_t[h - 2])

                def qbuild(E, r0=r0, ch=ch, pp=pp):
                    E.tensor_copy(qa[pp][0:64, :],
                                  q_all[r0:r0 + 64, ch * T:(ch + 1) * T]).then_inc(s_v.h, 1)

                b.emit("vector", qbuild)
                s_v.n += 1
                qa_q_v[h] = s_v.n
                qa_m_d[h] = dma("gpsimd", qa[pp][64:65, :], m_s.ap()[h:h + 1, :])

                if h > 0:
                    b.wait("tensor", s_sc, exp_last[h - 1])
                    b.wait("tensor", s_v, rb_ev)
                for half in range(2):
                    def vmm(E, h=h, half=half):
                        last = None
                        for k in range(16):
                            kc = half * 16 + k
                            last = E.matmul(psS[half][:, k * 64:(k + 1) * 64],
                                            ckv[0:64, kc * 128:(kc + 1) * 128],
                                            wvu_s[:, h * 64:(h + 1) * 64],
                                            start=True, stop=True)
                        last.then_inc(s_t.h, 1)

                    b.emit("tensor", vmm)
                    s_t.n += 1
                    b.wait("vector", s_t, s_t.n)
                    if h >= 2:
                        b.wait("vector", s_t, head_t[h - 2])

                    def vhev(E, pp=pp, half=half):
                        E.tensor_copy(
                            bass.AP(vh[pp], half * 16 * 65, [[32 * 65, 128], [65, 16], [1, 64]]),
                            psS[half].ap().rearrange("p (c f) -> p c f", c=16)).then_inc(s_v.h, 1)

                    b.emit("vector", vhev)
                    s_v.n += 1
                vh_v[h] = s_v.n

                b.wait("tensor", s_v, qa_q_v[h])
                b.wait("tensor", s_gd, qa_m_d[h])
                b.wait("tensor", s_v, vh_v[h])
                if h == 0:
                    b.wait("scalar", s_sc, p2a_done_sc) if False else None
                exp_sc = {}
                pv_t = {}

                def emit_S(qB, k):
                    gk = qB * 32 + k
                    kc = k

                    def smm(E, pp=pp, qB=qB, kc=kc):
                        E.matmul(psS[kc % 2][:, 0:512], ckv[:, kc * 128:(kc + 1) * 128],
                                 qa[pp][:, qB * 1024:qB * 1024 + 512], start=True, stop=True)
                        E.matmul(psS[kc % 2][:, 512:1024], ckv[:, kc * 128:(kc + 1) * 128],
                                 qa[pp][:, qB * 1024 + 512:(qB + 1) * 1024],
                                 start=True, stop=True).then_inc(s_t.h, 1)

                    b.emit("tensor", smm)
                    s_t.n += 1
                    b.wait("scalar", s_t, s_t.n)
                    if gk >= 2:
                        b.wait("scalar", s_t, pv_t[gk - 2])

                    def expv(E, kc=kc):
                        E.activation(pt[kc % 2][:], psS[kc % 2][:], AF.Exp).then_inc(s_sc.h, 1)

                    b.emit("scalar", expv)
                    s_sc.n += 1
                    exp_sc[gk] = s_sc.n

                emit_S(0, 0)
                if h > 0:
                    b.wait("tensor", s_v, rb_ev)
                emit_S(0, 1)
                for qB in range(4):
                    for k in range(32):
                        gk = qB * 32 + k
                        b.wait("tensor", s_sc, exp_sc[gk])
                        if k == 0:
                            b.wait("tensor", s_v, last_ytt.get(qB % 2, 0))
                            b.wait("tensor", s_v, rb_ev)

                        def pvmm(E, pp=pp, kc=k, qB=qB):
                            py = psY[qB % 2]
                            E.matmul(py[:, 0:512], vh[pp][:, kc * 65:(kc + 1) * 65],
                                     pt[kc % 2][:, 0:512], start=(kc == 0), stop=(kc == 31))
                            E.matmul(py[:, 512:1024], vh[pp][:, kc * 65:(kc + 1) * 65],
                                     pt[kc % 2][:, 512:1024],
                                     start=(kc == 0), stop=(kc == 31)).then_inc(s_t.h, 1)

                        b.emit("tensor", pvmm)
                        s_t.n += 1
                        pv_t[gk] = s_t.n
                        if k + 2 < 32:
                            emit_S(qB, k + 2)
                    b.wait("scalar", s_t, pv_t[qB * 32 + 31])

                    def rrowv(E, qB=qB, base=s_sc.n):
                        E.activation(rrow[:], psY[qB % 2][64:65, :], AF.Ln).then_inc(s_sc.h, 1)
                        E.wait_ge(s_sc.h, base + 1)
                        E.activation(rrow[:], rrow[:], AF.Exp, scale=-1.0).then_inc(s_sc.h, 1)

                    b.emit("scalar", rrowv)
                    s_sc.n += 2
                    rrow_sc = s_sc.n
                    if qB < 3:
                        emit_S(qB + 1, 0)
                        emit_S(qB + 1, 1)
                    b.wait("tensor", s_sc, rrow_sc)
                    b.wait("tensor", s_v, last_ytt.get((qB + 1) % 2, 0))

                    def rbmm(E, qB=qB):
                        po = psY[(qB + 1) % 2]
                        E.matmul(po[0:64, 0:512], ones1f[:, 0:64], rrow[:, 0:512],
                                 start=True, stop=True)
                        E.matmul(po[0:64, 512:1024], ones1f[:, 0:64], rrow[:, 512:1024],
                                 start=True, stop=True).then_inc(s_t.h, 1)

                    b.emit("tensor", rbmm)
                    s_t.n += 1
                    b.wait("vector", s_t, s_t.n)

                    def rbev(E, r0=r0, ch=ch, qB=qB, base=s_v.n):
                        E.tensor_copy(rbsb[:], psY[(qB + 1) % 2][0:64, :]).then_inc(s_v.h, 1)
                        E.wait_ge(s_v.h, base + 1)
                        E.tensor_tensor(
                            y_fm[r0:r0 + 64, ch * T + qB * 1024: ch * T + (qB + 1) * 1024],
                            psY[qB % 2][0:64, :], rbsb[:], OP.mult).then_inc(s_v.h, 1)

                    b.emit("vector", rbev)
                    s_v.n += 2
                    rb_ev = s_v.n - 1
                    last_ytt[qB % 2] = s_v.n
                exp_last[h] = s_sc.n
                head_t[h] = s_t.n
            exp_sc_last = exp_last[H - 1]

            pj_sc = {}
            pj_d = {}
            for g in range(64):
                tq, mc = g // 8, g % 8
                if g == 0:
                    b.wait("tensor", s_v, last_ytt.get(0, 0))
                    b.wait("tensor", s_v, last_ytt.get(1, 0))
                    b.wait("tensor", s_v, rb_ev)
                    b.wait("tensor", s_sc, exp_sc_last)
                if g >= 2:
                    b.wait("tensor", s_sc, pj_sc[g - 2])

                def pjmm(E, tq=tq, mc=mc, g=g):
                    last = None
                    for kv in range(8):
                        last = E.matmul(psS[g % 2][:, 0:512],
                                        wpj_s[:, (kv * 8 + mc) * 128:(kv * 8 + mc + 1) * 128],
                                        y_fm[:, kv * T + tq * 512: kv * T + (tq + 1) * 512],
                                        start=(kv == 0), stop=(kv == 7))
                    last.then_inc(s_t.h, 1)

                b.emit("tensor", pjmm)
                s_t.n += 1
                b.wait("scalar", s_t, s_t.n)
                if g >= 2:
                    b.wait("scalar", s_sd, pj_d[g - 2])

                def pjev(E, mc=mc, g=g):
                    E.activation(sstg[:, (g % 2) * 512:(g % 2 + 1) * 512],
                                 psS[g % 2][:, 0:512], AF.Identity,
                                 bias=btot[:, mc:mc + 1]).then_inc(s_sc.h, 1)

                b.emit("scalar", pjev)
                s_sc.n += 1
                pj_sc[g] = s_sc.n
                b.wait("sync", s_sc, s_sc.n)
                pj_d[g] = dma("sync", s_s.ap()[:, mc * T + tq * 512: mc * T + (tq + 1) * 512],
                              sstg[:, (g % 2) * 512:(g % 2 + 1) * 512])
            b.barrier([s_t, s_sc, s_v, s_gd, s_sd])
        es_ab.close()

        if phases < 4:
            _checkpoint()
            return nc

        with contextlib.ExitStack() as es3:
            sb3 = lambda n, s, d: es3.enter_context(nc.sbuf_tensor(n, s, d, side="right"))
            hten = sb3("hten", [128, 32 * TCH], BF16)
            x_c = sb3("x_c", [128, 8 * TCH], F32)
            xb = sb3("xb", [128, 8 * TCH], BF16)
            xhat = sb3("xhat", [128, 8 * TCH], BF16)
            w1s = sb3("w1s", [128, 2 * 2048], BF16)
            w2s = sb3("w2s", [128, 2 * 2048], BF16)
            b1s = sb3("b1s", [128, NL * NE * 32], F32)
            b2s = sb3("b2s", [128, NL * NE * 8], F32)
            rws = sb3("rws", [128, NL * 8 * 4], F32)
            rbs = sb3("rbs", [1, NL * 4], F32)
            rbsbf = sb3("rbsbf", [1, NL * 4], BF16)
            gcs = sb3("gcs", [128, NL * NE * 8], F32)
            bcs = sb3("bcs", [128, NL * NE * 8], F32)
            oh4s = sb3("oh4s", [4, 640], F32)
            sqtmp = sb3("sqtmp", [128, 8 * TCH], BF16)
            lgt = sb3("lgt", [128, 16], F32)
            comb5 = sb3("comb5", [128, 8], F32)
            comb_all = sb3("comb_all", [128, 32], F32)
            c4 = sb3("c4", [4, TCH], F32)
            rowA = sb3("rowA", [1, TCH], F32)
            rowB = sb3("rowB", [1, TCH], F32)
            rowC = sb3("rowC", [1, TCH], F32)
            rbb_all = sb3("rbb_all", [128, 16], F32)
            cb_b = sb3("cb_b", [128, TCH], F32)
            tmpv = sb3("tmpv", [128, 2 * 512], F32)
            sti = sb3("sti", [128, TCH], F32)
            gbuf = sb3("gbuf", [128, TCH], F32)
            outsb = sb3("outsb", [128, 1024], F32)
            psh = [es3.enter_context(nc.psum_tensor("psh0", [128, 1024], F32)),
                   es3.enter_context(nc.psum_tensor("psh1", [128, 1024], F32))]
            psy = [es3.enter_context(nc.psum_tensor("psy0", [128, 512], F32)),
                   es3.enter_context(nc.psum_tensor("psy1", [128, 512], F32))]
            psm = es3.enter_context(nc.psum_tensor("psm", [128, 1024], F32))

            for d_, s_ in ((b1s, b1c), (b2s, b2c), (rws, rwl), (rbs, rbr),
                           (gcs, gcl), (bcs, bcl), (oh4s, oh4)):
                static_ffn = dma("gpsimd", d_[:], s_.ap())
            b.wait("vector", s_gd, static_ffn)
            b.emit("vector", lambda E: E.tensor_copy(rbsbf[:], rbs[:]).then_inc(s_v.h, 1))
            s_v.n += 1
            static_v = s_v.n

            b.wait("tensor", s_v, static_v)

            def rbbmm(E):
                E.matmul(psm[:, 0:16], ones1b[:], rbsbf[:, 0:16],
                         start=True, stop=True).then_inc(s_t.h, 1)

            b.emit("tensor", rbbmm)
            s_t.n += 1
            b.wait("vector", s_t, s_t.n)

            def rbbev(E):
                E.tensor_copy(rbb_all[:], psm[:, 0:16]).then_inc(s_v.h, 1)

            b.emit("vector", rbbev)
            s_v.n += 1
            psm_free_v = s_v.n

            wstart = s_gd.n
            nblk = NCH * NL * NE * 32
            wblk_n = [wstart + 16 * (g + 1) for g in range(nblk)]
            s_gd.n = wstart + 16 * nblk
            blk_done = {}
            combine_done_v = {}

            def emit_weight_dmas(c, l, e):
                eg = (c * NL + l) * NE + e
                for i in range(32):
                    fam = 0 if i < 16 else 1
                    fi = eg * 16 + (i if i < 16 else i - 16)
                    pg = -1
                    if fi >= 2:
                        pg = (fi - 2) // 16 * 32 + (0 if fam == 0 else 16) + (fi - 2) % 16
                    guard = blk_done[pg] if pg >= 0 else 0
                    vguard = combine_done_v.get(c - 1, 0) if (l == 0 and e == 0) else 0
                    if fam == 0:
                        srcap = w1l.ap()[l, e, i, :, :]
                        dst = w1s[:, (i % 2) * 2048:(i % 2 + 1) * 2048]
                    else:
                        srcap = w2l.ap()[l, e, i - 16, :, :]
                        dst = w2s[:, (i % 2) * 2048:(i % 2 + 1) * 2048]

                    def wdma(E, d=dst, s=srcap, g=guard, vg=vguard):
                        if g:
                            E.wait_ge(s_t.h, g)
                        if vg:
                            E.wait_ge(s_v.h, vg)
                        E.dma_start(d, s).then_inc(s_gd.h, 16)

                    b.emit("gpsimd", wdma)

            h_evac = {}
            y_evac = {}
            hlp = {"sx_last_t": 0, "rowev_v": 0, "wrow_v": 0}

            def emit_sxk(kd, last):
                if kd == 0:
                    b.wait("tensor", s_v, psm_free_v)
                b.wait("tensor", s_v, sq_v[kd])

                def sxk(E, kd=kd, last=last):
                    lm = None
                    for tsc in range(2):
                        E.matmul(psm[0:1, tsc * 512:(tsc + 1) * 512], ones128b[:],
                                 xb[:, kd * TCH + tsc * 512: kd * TCH + (tsc + 1) * 512],
                                 start=(kd == 0), stop=last)
                        lm = E.matmul(psm[64:65, tsc * 512:(tsc + 1) * 512], ones128b[:],
                                      sqtmp[:, kd * TCH + tsc * 512: kd * TCH + (tsc + 1) * 512],
                                      start=(kd == 0), stop=last)
                    lm.then_inc(s_t.h, 1)

                b.emit("tensor", sxk)
                s_t.n += 1
                if last:
                    hlp["sx_last_t"] = s_t.n

            def emit_rows():
                b.wait("vector", s_t, hlp["sx_last_t"])

                def rowev(E, base=s_v.n):
                    E.tensor_copy(rowA[:], psm[0:1, 0:1024]).then_inc(s_v.h, 1)
                    E.wait_ge(s_v.h, base + 1)
                    E.tensor_copy(rowB[:], psm[64:65, 0:1024]).then_inc(s_v.h, 1)

                b.emit("vector", rowev)
                s_v.n += 2
                hlp["rowev_v"] = s_v.n

                def murow(E, base=s_v.n):
                    E.tensor_scalar(rowA[:], rowA[:], 1.0 / D, None, OP.mult, OP.bypass)
                    E.tensor_scalar(rowB[:], rowB[:], 1.0 / D, None, OP.mult, OP.bypass).then_inc(s_v.h, 1)
                    E.wait_ge(s_v.h, base + 1)
                    E.tensor_tensor(rowC[:], rowA[:], rowA[:], OP.mult).then_inc(s_v.h, 1)
                    E.wait_ge(s_v.h, base + 2)
                    E.tensor_tensor(rowB[:], rowB[:], rowC[:], OP.subtract).then_inc(s_v.h, 1)

                b.emit("vector", murow)
                s_v.n += 3
                b.wait("scalar", s_v, s_v.n)

                def rstdrow(E, base=s_sc.n):
                    E.activation(rowC[:], rowB[:], AF.Ln, bias=epsc[0:1, :]).then_inc(s_sc.h, 1)
                    E.wait_ge(s_sc.h, base + 1)
                    E.activation(rowC[:], rowC[:], AF.Exp, scale=-0.5).then_inc(s_sc.h, 1)

                b.emit("scalar", rstdrow)
                s_sc.n += 2
                b.wait("vector", s_sc, s_sc.n)

                def wrowv(E):
                    E.tensor_tensor(rowB[:], rowA[:], rowC[:], OP.mult).then_inc(s_v.h, 1)

                b.emit("vector", wrowv)
                s_v.n += 1
                hlp["wrow_v"] = s_v.n

            def emit_sums_rows():
                for kd in range(8):
                    emit_sxk(kd, kd == 7)
                emit_rows()

            prtr_last_v = 0

            def emit_l0_rows(cc, war_t, war_v):
                if war_v:
                    b.wait("sync", s_v, war_v)
                if war_t:
                    b.wait("sync", s_t, war_t)
                vd = dma("sync", rowA[:], bass.AP(v_s, cc * TCH, [[1, 1], [1, TCH]]))
                b.wait("vector", s_sd, vd)

                def l0va(E, base=s_v.n):
                    E.tensor_scalar(rowB[:], rowA[:], EPS, None, OP.add, OP.bypass).then_inc(s_v.h, 1)
                    E.wait_ge(s_v.h, base + 1)
                    E.reciprocal(rowB[:], rowB[:]).then_inc(s_v.h, 1)
                    E.wait_ge(s_v.h, base + 2)
                    E.tensor_tensor(rowB[:], rowA[:], rowB[:], OP.mult).then_inc(s_v.h, 1)

                b.emit("vector", l0va)
                s_v.n += 3
                b.wait("scalar", s_v, s_v.n)

                def l0rstd(E, base=s_sc.n):
                    E.activation(rowC[:], rowB[:], AF.Ln, bias=epsc[0:1, :]).then_inc(s_sc.h, 1)
                    E.wait_ge(s_sc.h, base + 1)
                    E.activation(rowC[:], rowC[:], AF.Exp, scale=-0.5).then_inc(s_sc.h, 1)

                b.emit("scalar", l0rstd)
                s_sc.n += 2
                b.wait("vector", s_sc, s_sc.n)

                def l0wz(E):
                    E.memset(rowB[:], 0.0).then_inc(s_v.h, 1)

                b.emit("vector", l0wz)
                s_v.n += 1
                hlp["wrow_v"] = s_v.n
                hlp["rowev_v"] = s_v.n
            out_dma_prev = 0
            xb_free_t = 0
            cb_ev_last = 0
            cb_t_last = 0
            sq_v = {}
            pend_stats = False
            pend_sums = False

            emit_l0_rows(0, 0, 0)
            pend_sums = True
            xc_done = dma("sync", x_c.ap().rearrange("p (k t) -> p k t", k=8),
                          bass.AP(xn_s, 0, [[8 * T, 128], [T, 8], [1, TCH]]))
            for c in range(NCH):
                s_dma = {}
                prod_v = {}
                for l in range(NL):
                    li = c * NL + l
                    if not pend_stats and not pend_sums:
                        b.wait("vector", s_sd, xc_done)
                        b.wait("vector", s_t, xb_free_t)
                        for kd in range(8):
                            def xbsq(E, kd=kd, base=s_v.n):
                                E.tensor_copy(xb[:, kd * TCH:(kd + 1) * TCH],
                                              x_c[:, kd * TCH:(kd + 1) * TCH]).then_inc(s_v.h, 1)
                                E.wait_ge(s_v.h, base + 1)
                                E.tensor_tensor(sqtmp[:, kd * TCH:(kd + 1) * TCH],
                                                xb[:, kd * TCH:(kd + 1) * TCH],
                                                xb[:, kd * TCH:(kd + 1) * TCH],
                                                OP.mult).then_inc(s_v.h, 1)

                            b.emit("vector", xbsq)
                            s_v.n += 2
                            sq_v[kd] = s_v.n
                    pend_stats = False

                    if not pend_sums:
                        emit_sums_rows()
                    pend_sums = False

                    b.wait("tensor", s_v, hlp["wrow_v"])
                    if li > 0:
                        b.wait("tensor", s_sc, h_evac[li * NE * 32 - 1])
                    if c > 0 and l == 0:
                        b.wait("tensor", s_v, prtr_last_v)
                    for rowt, dstp in ((rowC, psh[0]), (rowB, psh[1])):
                        def bcmm(E, rowt=rowt, dstp=dstp):
                            E.matmul(dstp[:, 0:512], ones1f[:], rowt[:, 0:512], start=True, stop=True)
                            E.matmul(dstp[:, 512:1024], ones1f[:], rowt[:, 512:1024],
                                     start=True, stop=True).then_inc(s_t.h, 1)

                        b.emit("tensor", bcmm)
                        s_t.n += 1
                    bc_t = s_t.n

                    b.wait("vector", s_t, bc_t)
                    if l == 0:
                        b.wait("vector", s_sd, xc_done)
                    for kd in range(8):
                        def xhatk(E, kd=kd, base=s_v.n):
                            E.tensor_tensor(tmpv[:, 0:512],
                                            x_c[:, kd * TCH: kd * TCH + 512], psh[0][:, 0:512], OP.mult)
                            E.tensor_tensor(tmpv[:, 512:1024],
                                            x_c[:, kd * TCH + 512:(kd + 1) * TCH],
                                            psh[0][:, 512:1024], OP.mult).then_inc(s_v.h, 1)
                            E.wait_ge(s_v.h, base + 1)
                            E.tensor_tensor(xhat[:, kd * TCH:(kd + 1) * TCH],
                                            tmpv[:, 0:1024], psh[1][:], OP.subtract).then_inc(s_v.h, 1)

                        b.emit("vector", xhatk)
                        s_v.n += 2
                    xhat_v = s_v.n

                    comb_v = {}
                    comb2_v = {}
                    ctr_v = {}
                    rt_t = {}
                    xsc_v_cur = 0
                    cb_v = 0
                    for e in range(NE):
                        eg = li * NE + e
                        col = l * NE + e
                        if e == 0:
                            b.wait("vector", s_t, hlp["sx_last_t"])
                        else:
                            b.wait("vector", s_t, hmm_last_t)

                        def xev(E, col=col):
                            last = None
                            for kd in range(8):
                                last = E.tensor_scalar(xb[:, kd * TCH:(kd + 1) * TCH],
                                                       xhat[:, kd * TCH:(kd + 1) * TCH],
                                                       gcs[:, col * 8 + kd:col * 8 + kd + 1],
                                                       bcs[:, col * 8 + kd:col * 8 + kd + 1],
                                                       OP.mult, OP.add)
                            last.then_inc(s_v.h, 1)

                        b.emit("vector", xev)
                        s_v.n += 1
                        xe_v = s_v.n

                        for fb in range(16):
                            b.wait("tensor", s_gd, wblk_n[eg * 32 + fb])
                            slot = fb % 2
                            for mf in range(2):
                                gmf = fb * 2 + mf
                                gidx = eg * 32 + gmf
                                ph = psh[gidx % 2]
                                if gidx >= 2:
                                    b.wait("tensor", s_sc, h_evac.get(gidx - 2, 0))
                                if fb == 0 and mf == 0:
                                    if e == 0:
                                        b.wait("tensor", s_v, xhat_v)
                                    b.wait("tensor", s_v, xe_v)

                                def hmm(E, slot=slot, mf=mf, ph=ph):
                                    last = None
                                    for kd in range(8):
                                        lhs = w1s[:, slot * 2048 + kd * 256 + mf * 128:
                                                  slot * 2048 + kd * 256 + (mf + 1) * 128]
                                        for tsc in range(2):
                                            last = E.matmul(
                                                ph[:, tsc * 512:(tsc + 1) * 512], lhs,
                                                xb[:, kd * TCH + tsc * 512: kd * TCH + (tsc + 1) * 512],
                                                start=(kd == 0), stop=(kd == 7))
                                    last.then_inc(s_t.h, 1)

                                b.emit("tensor", hmm)
                                s_t.n += 1
                                b.wait("scalar", s_t, s_t.n)
                                if eg == 0 and gidx == 0:
                                    b.wait("scalar", s_gd, static_ffn)

                                def geluv(E, gmf=gmf, ph=ph, col=col):
                                    E.activation(hten[:, gmf * TCH:(gmf + 1) * TCH], ph[:],
                                                 AF.Gelu,
                                                 bias=b1s[:, col * 32 + gmf:col * 32 + gmf + 1]
                                                 ).then_inc(s_sc.h, 1)

                                b.emit("scalar", geluv)
                                s_sc.n += 1
                                h_evac[gidx] = s_sc.n
                            blk_done[eg * 32 + fb] = s_t.n

                            if e == 0 and fb < 8:
                                jt = fb
                                if jt == 0:
                                    b.wait("tensor", s_v, hlp["rowev_v"])
                                    b.wait("tensor", s_v, psm_free_v)
                                    if l == 0:
                                        b.wait("tensor", s_sd, xc_done)
                                else:
                                    b.wait("tensor", s_v, comb_v[jt - 1])

                                def rtmm(E, l=l, jt=jt):
                                    last = None
                                    for kd in range(8):
                                        last = E.matmul(psm[:, 8:12],
                                                        x_c[:, kd * TCH + jt * 128: kd * TCH + (jt + 1) * 128],
                                                        rws[:, (l * 8 + kd) * 4:(l * 8 + kd + 1) * 4],
                                                        start=(kd == 0), stop=(kd == 7))
                                    last.then_inc(s_t.h, 1)

                                b.emit("tensor", rtmm)
                                s_t.n += 1
                                rt_t[jt] = s_t.n
                                b.wait("vector", s_t, rt_t[jt])

                                def combv(E, l=l, base=s_v.n):
                                    E.tensor_tensor(lgt[:, 0:4], psm[:, 8:12],
                                                    rbb_all[:, l * 4:(l + 1) * 4], OP.add).then_inc(s_v.h, 1)
                                    E.wait_ge(s_v.h, base + 1)
                                    E.reduce_max(lgt[:, 4:5], lgt[:, 0:4], axis=AX.X, negate=True).then_inc(s_v.h, 1)

                                b.emit("vector", combv)
                                s_v.n += 2
                                comb_v[jt] = s_v.n
                                b.wait("scalar", s_v, comb_v[jt])

                                def combe(E):
                                    E.activation(lgt[:, 8:12], lgt[:, 0:4], AF.Exp, bias=lgt[:, 4:5]).then_inc(s_sc.h, 1)

                                b.emit("scalar", combe)
                                s_sc.n += 1
                                b.wait("vector", s_sc, s_sc.n)

                                def combv2(E, jt=jt, base=s_v.n):
                                    el = lgt[:, 8:12]
                                    E.reduce_max(lgt[:, 5:6], el, axis=AX.X).then_inc(s_v.h, 1)
                                    E.wait_ge(s_v.h, base + 1)
                                    E.tensor_scalar(lgt[:, 12:16], el, lgt[:, 5:6], None, OP.is_ge, OP.bypass).then_inc(s_v.h, 1)
                                    E.wait_ge(s_v.h, base + 2)
                                    E.tensor_tensor(comb5[:, 4:8], el, lgt[:, 12:16], OP.mult).then_inc(s_v.h, 1)
                                    E.wait_ge(s_v.h, base + 3)
                                    E.tensor_tensor(comb5[:, 4:8], el, comb5[:, 4:8], OP.subtract).then_inc(s_v.h, 1)
                                    E.wait_ge(s_v.h, base + 4)
                                    E.reduce_max(lgt[:, 6:7], comb5[:, 4:8], axis=AX.X).then_inc(s_v.h, 1)
                                    E.wait_ge(s_v.h, base + 5)
                                    E.tensor_scalar(comb5[:, 4:8], el, lgt[:, 6:7], None, OP.is_ge, OP.bypass)
                                    E.tensor_scalar(lgt[:, 7:8], lgt[:, 6:7], 1.0, None, OP.add, OP.bypass).then_inc(s_v.h, 1)
                                    E.wait_ge(s_v.h, base + 6)
                                    E.reciprocal(lgt[:, 7:8], lgt[:, 7:8]).then_inc(s_v.h, 1)
                                    E.wait_ge(s_v.h, base + 7)
                                    E.scalar_tensor_tensor(comb_all[:, jt * 4:(jt + 1) * 4], el,
                                                           lgt[:, 7:8], comb5[:, 4:8],
                                                           OP.mult, OP.mult).then_inc(s_v.h, 1)

                                b.emit("vector", combv2)
                                s_v.n += 8
                                comb2_v[jt] = s_v.n

                            if e == 0 and 2 <= fb < 10:
                                jt = fb - 2
                                b.wait("tensor", s_v, comb2_v[jt])

                                def ctr(E, jt=jt):
                                    E.transpose(psm[0:4, 512:640], comb_all[:, jt * 4:(jt + 1) * 4],
                                                idn_s[:]).then_inc(s_t.h, 1)

                                b.emit("tensor", ctr)
                                s_t.n += 1
                                b.wait("vector", s_t, s_t.n)
                                if jt == 0:
                                    b.wait("vector", s_t, cb_t_last)

                                def ctrev(E, jt=jt):
                                    E.tensor_copy(c4[0:4, jt * 128:(jt + 1) * 128],
                                                  psm[0:4, 512:640]).then_inc(s_v.h, 1)

                                b.emit("vector", ctrev)
                                s_v.n += 1
                                ctr_v[jt] = s_v.n

                            if e == 0 and fb == 10:
                                b.wait("tensor", s_v, ctr_v[7])
                                b.wait("tensor", s_v, comb_v[7])

                                def csbmm(E):
                                    E.matmul(psm[:, 0:512], oh4s[:, 512:640], c4[0:4, 0:512],
                                             start=True, stop=True)
                                    E.matmul(psm[:, 512:1024], oh4s[:, 512:640], c4[0:4, 512:1024],
                                             start=True, stop=True).then_inc(s_t.h, 1)

                                b.emit("tensor", csbmm)
                                s_t.n += 1
                                csb_t = s_t.n
                                b.wait("vector", s_t, csb_t)
                                for kd in range(8):
                                    def xscale(E, kd=kd):
                                        E.tensor_tensor(x_c[:, kd * TCH:(kd + 1) * TCH],
                                                        x_c[:, kd * TCH:(kd + 1) * TCH],
                                                        psm[:, 0:1024], OP.mult).then_inc(s_v.h, 1)

                                    b.emit("vector", xscale)
                                    s_v.n += 1
                                xsc_v_cur = s_v.n

                            if fb == 12:
                                if e == 0:
                                    b.wait("tensor", s_v, xsc_v_cur)
                                else:
                                    b.wait("tensor", s_v, cb_ev_last)

                                def cbmm(E, e=e):
                                    E.matmul(psm[:, 0:512], oh4s[:, e * 128:(e + 1) * 128],
                                             c4[0:4, 0:512], start=True, stop=True)
                                    E.matmul(psm[:, 512:1024], oh4s[:, e * 128:(e + 1) * 128],
                                             c4[0:4, 512:1024], start=True, stop=True).then_inc(s_t.h, 1)

                                b.emit("tensor", cbmm)
                                s_t.n += 1
                                cb_t_last = s_t.n
                                b.wait("vector", s_t, cb_t_last)

                                def cbev(E):
                                    E.tensor_copy(cb_b[:], psm[:, 0:1024]).then_inc(s_v.h, 1)

                                b.emit("vector", cbev)
                                s_v.n += 1
                                cb_v = s_v.n
                                cb_ev_last = cb_v
                        hmm_last_t = s_t.n

                        is_last_e = (e == NE - 1)
                        nxt_layer = is_last_e and l < NL - 1
                        tail_here = is_last_e and l == NL - 1
                        for md in range(8):
                            if md == 0:
                                b.wait("tensor", s_v, cb_v)
                            for hf in range(2):
                                if md == 0:
                                    b.wait("tensor", s_sc,
                                           h_evac[eg * 32 + (15 if hf == 0 else 31)])
                                blkidx = eg * 32 + 16 + md * 2 + hf
                                b.wait("tensor", s_gd, wblk_n[blkidx])
                                slot = (md * 2 + hf) % 2
                                for tsc in range(2):
                                    gy = eg * 16 + md * 2 + tsc
                                    py = psy[gy % 2]
                                    if hf == 0 and gy >= 2:
                                        b.wait("tensor", s_v, y_evac.get(gy - 2, 0))

                                    def ymm(E, slot=slot, py=py, tsc=tsc, hf=hf):
                                        last = None
                                        for k in range(16):
                                            lhs = w2s[:, slot * 2048 + k * 128:slot * 2048 + (k + 1) * 128]
                                            last = E.matmul(
                                                py[:], lhs,
                                                hten[:, (hf * 16 + k) * TCH + tsc * 512:
                                                     (hf * 16 + k) * TCH + (tsc + 1) * 512],
                                                start=(hf == 0 and k == 0), stop=(hf == 1 and k == 15))
                                        last.then_inc(s_t.h, 1)

                                    b.emit("tensor", ymm)
                                    s_t.n += 1
                                    if hf == 1:
                                        b.wait("vector", s_t, s_t.n)

                                        def yev(E, md=md, tsc=tsc, py=py, gy=gy, col=col, base=s_v.n):
                                            tp = tmpv[:, (gy % 2) * 512:(gy % 2 + 1) * 512]
                                            E.scalar_tensor_tensor(tp, py[:],
                                                                   b2s[:, col * 8 + md:col * 8 + md + 1],
                                                                   cb_b[:, tsc * 512:(tsc + 1) * 512],
                                                                   OP.add, OP.mult).then_inc(s_v.h, 1)
                                            E.wait_ge(s_v.h, base + 1)
                                            sl = x_c[:, md * TCH + tsc * 512: md * TCH + (tsc + 1) * 512]
                                            E.tensor_tensor(sl, sl, tp, OP.add).then_inc(s_v.h, 1)

                                        b.emit("vector", yev)
                                        s_v.n += 2
                                        y_evac[gy] = s_v.n - 1

                                        if nxt_layer and tsc == 1:
                                            kd = md
                                            if kd == 0:
                                                b.wait("vector", s_t, hmm_last_t)

                                            def xbsq(E, kd=kd, base=s_v.n):
                                                E.tensor_copy(xb[:, kd * TCH:(kd + 1) * TCH],
                                                              x_c[:, kd * TCH:(kd + 1) * TCH]).then_inc(s_v.h, 1)
                                                E.wait_ge(s_v.h, base + 1)
                                                E.tensor_tensor(sqtmp[:, kd * TCH:(kd + 1) * TCH],
                                                                xb[:, kd * TCH:(kd + 1) * TCH],
                                                                xb[:, kd * TCH:(kd + 1) * TCH],
                                                                OP.mult).then_inc(s_v.h, 1)

                                            b.emit("vector", xbsq)
                                            s_v.n += 2
                                            sq_v[kd] = s_v.n
                                blk_done[blkidx] = s_t.n
                            if nxt_layer and md >= 1:
                                emit_sxk(md - 1, False)
                            if tail_here:
                                kd = md
                                if kd >= 1:
                                    b.wait("sync", s_v, prod_v[kd - 1])
                                s_dma[kd] = dma("sync", sti[:, 0:TCH],
                                                bass.AP(s_s, kd * T + c * TCH, [[8 * T, 128], [1, TCH]]))
                                b.wait("scalar", s_v, y_evac[eg * 16 + kd * 2 + 1] + 1)
                                if kd >= 1:
                                    b.wait("scalar", s_v, prod_v[kd - 1])

                                def siluk(E, kd=kd):
                                    E.activation(gbuf[:, 0:TCH],
                                                 x_c[:, kd * TCH:(kd + 1) * TCH],
                                                 AF.Silu).then_inc(s_sc.h, 1)

                                b.emit("scalar", siluk)
                                s_sc.n += 1
                                b.wait("vector", s_sc, s_sc.n)
                                b.wait("vector", s_sd, s_dma[kd])

                                def prodk(E, kd=kd):
                                    E.tensor_tensor(x_c[:, kd * TCH:(kd + 1) * TCH],
                                                    sti[:, 0:TCH],
                                                    gbuf[:, 0:TCH],
                                                    OP.mult).then_inc(s_v.h, 1)

                                b.emit("vector", prodk)
                                s_v.n += 1
                                prod_v[kd] = s_v.n
                        if nxt_layer:
                            emit_sxk(7, True)
                            emit_rows()
                            pend_sums = True
                        xb_free_t = hmm_last_t
                        emit_weight_dmas(c, l, e)
                    psm_free_v = cb_ev_last

                if c < NCH - 1:
                    emit_l0_rows(c + 1, bc_t, hlp["wrow_v"])
                    pend_sums = True

                out_v = {}
                xi_dma = {}
                for jt in range(8):
                    gjt = c * 8 + jt
                    if jt >= 1 or c > 0:
                        b.wait("sync", s_sd, out_dma_prev)
                    xi_dma[jt] = dma("sync", outsb[:],
                                     x_in.ap()[gjt * 128:(gjt + 1) * 128, :])
                    for half in range(2):
                        b.wait("tensor", s_v, prod_v[7])
                        if jt >= 1 or half >= 1:
                            b.wait("tensor", s_v, prtr_ev)
                        if jt == 0 and half == 0:
                            b.wait("tensor", s_sc, h_evac[((c * NL + NL - 1) * NE + NE - 1) * 32 + 31])

                        def prtr(E, jt=jt, half=half):
                            last = None
                            for k in range(4):
                                kd = half * 4 + k
                                last = E.transpose(psh[half][:, k * 128:(k + 1) * 128],
                                                   x_c[:, kd * TCH + jt * 128: kd * TCH + (jt + 1) * 128],
                                                   idn_s[:])
                            last.then_inc(s_t.h, 1)

                        b.emit("tensor", prtr)
                        s_t.n += 1
                        b.wait("vector", s_t, s_t.n)
                        b.wait("vector", s_sd, xi_dma[jt])

                        def prtrev(E, half=half):
                            last = None
                            for k in range(4):
                                kd = half * 4 + k
                                last = E.tensor_tensor(
                                    outsb[:, kd * 128:(kd + 1) * 128],
                                    outsb[:, kd * 128:(kd + 1) * 128],
                                    psh[half][:, k * 128:(k + 1) * 128], OP.add)
                            last.then_inc(s_v.h, 1)

                        b.emit("vector", prtrev)
                        s_v.n += 1
                        prtr_ev = s_v.n
                    prtr_jt_t = s_t.n
                    out_v[jt] = s_v.n
                    b.wait("sync", s_v, out_v[jt])
                    od = dma("sync", out.ap()[gjt * 128:(gjt + 1) * 128, :], outsb[:])
                    out_dma_prev = od
                    if c < NCH - 1:
                        b.wait("scalar", s_t, prtr_jt_t)
                        xc_done = dma(
                            "scalar",
                            x_c.ap().rearrange("p (k t) -> p k t", k=8)[:, :, jt * 128:(jt + 1) * 128],
                            bass.AP(xn_s, (c + 1) * TCH + jt * 128,
                                    [[8 * T, 128], [T, 8], [1, 128]]))
                prtr_last_v = prtr_ev
                combine_done_v[c] = prtr_ev

            b.wait("sync", s_sd, s_sd.n)

        blk = es0.enter_context(nc.Block())
        b.run(blk)
    return nc


def kernel(**inputs):
    f32 = np.float32
    g = lambda k: np.asarray(inputs[k], dtype=f32)
    x = g("L0_attn_x")
    wq_d = g("L0_attn_wq_d")
    bq_d = g("L0_attn_bq_d")
    w_qk = g("L0_attn_w_qk")
    b_qk = g("L0_attn_b_qk")
    wkv_d = g("L0_attn_wkv_d")
    bkv_d = g("L0_attn_bkv_d")
    wv_u = g("L0_attn_wv_u")
    bv_u = g("L0_attn_bv_u")
    w_proj = g("L0_attn_w_proj")
    b_proj = g("L0_attn_b_proj")
    rw = g("L0_ffn_router_w")
    rb = g("L0_ffn_router_b")
    ln_g = g("L0_ffn_ln_g")
    ln_b = g("L0_ffn_ln_b")
    w1 = g("L0_ffn_w1")
    b1 = g("L0_ffn_b1")
    w2 = g("L0_ffn_w2")
    b2 = g("L0_ffn_b2")

    C = np.ascontiguousarray

    def _mk_oh4():
        oh = np.zeros((4, 640), dtype=f32)
        for e in range(4):
            oh[e, e * 128:(e + 1) * 128] = 1.0
        oh[:, 512:640] = 1.0
        return oh

    feed = {
        "idn": np.eye(128, dtype=f32),
        "wqd": C(wq_d.T.reshape(8, 128, 4, 128).transpose(1, 0, 2, 3).reshape(128, 4096)),
        "wqk": C(w_qk.T.reshape(4, 128, 16, 64).transpose(1, 0, 2, 3).reshape(128, 4096)),
        "wkv": C(wkv_d.T.reshape(8, 128, 64).transpose(1, 0, 2).reshape(128, 512)),
        "wvu": C(wv_u.T),
        "wpj": C(w_proj.T.reshape(8, 128, 8, 128).transpose(1, 0, 2, 3).reshape(128, 8192)),
        "bqd": C(bq_d.reshape(4, 128).T),
        "bqk": C(b_qk.reshape(16, 64).T),
        "bkv": C(bkv_d.reshape(64, 1)),
        "bvu": C(bv_u.reshape(8, 128).T),
        "bpj": C(b_proj.reshape(8, 128).T),
        "w1l": C(w1.reshape(NL, NE, 8, 128, 16, 256).transpose(0, 1, 4, 3, 2, 5)
                 .reshape(NL, NE, 16, 128, 2048)),
        "w2l": C(w2.reshape(NL, NE, 2, 16, 128, 8, 128).transpose(0, 1, 5, 2, 4, 3, 6)
                 .reshape(NL, NE, 16, 128, 2048)),
        "b1c": C(b1.reshape(NL, NE, 32, 128).transpose(3, 0, 1, 2).reshape(128, NL * NE * 32)),
        "b2c": C(b2.reshape(NL, NE, 8, 128).transpose(3, 0, 1, 2).reshape(128, NL * NE * 8)),
        "rwl": C(rw.reshape(NL, 8, 128, 4).transpose(2, 0, 1, 3).reshape(128, NL * 8 * 4)),
        "rbr": C(rb.reshape(1, NL * 4)),
        "gcl": C(ln_g.reshape(NL, NE, 8, 128).transpose(3, 0, 1, 2).reshape(128, NL * NE * 8)),
        "oh4": _mk_oh4(),
        "bcl": C(ln_b.reshape(NL, NE, 8, 128).transpose(3, 0, 1, 2).reshape(128, NL * NE * 8)),
    }

    nc = bass.Bass("TRN2", target_bir_lowering=False)
    build(nc)
    in_maps = []
    for core in range(8):
        m = dict(feed)
        m["x_in"] = C(x[core])
        in_maps.append(m)
    res = run_bass_kernel_spmd(nc, in_maps, core_ids=list(range(8)))
    return np.stack([np.asarray(r["out"]) for r in res.results], axis=0)



# version 09
# speedup vs baseline: 1.0849x; 1.0849x over previous
import os
import contextlib
import numpy as np
import concourse.bass as bass
import concourse.mybir as mybir
from concourse.bass_utils import run_bass_kernel_spmd
from concourse.alu_op_type import AluOpType

F32 = mybir.dt.float32
BF16 = mybir.dt.bfloat16
AF = mybir.ActivationFunctionType
AX = mybir.AxisListType
OP = AluOpType

T = 4096
D = 1024
H = 16
FF = 4096
NL = 4
NE = 4
TCH = 1024
NCH = T // TCH
EPS = 1e-5


class Sem:
    def __init__(self, h):
        self.h = h
        self.n = 0


class Bld:
    ENGS = ("sync", "scalar", "vector", "tensor", "gpsimd")

    def __init__(self):
        self.streams = {e: [] for e in self.ENGS}

    def emit(self, eng, fn):
        self.streams[eng].append(fn)

    def wait(self, eng, sem, tgt):
        if tgt > 0:
            self.emit(eng, lambda E, s=sem.h, t=tgt: E.wait_ge(s, t))

    def barrier(self, sems):
        for eng in self.ENGS:
            for s in sems:
                self.wait(eng, s, s.n)

    def run(self, block):
        for eng in self.ENGS:
            fns = self.streams[eng]
            if not fns:
                continue

            def body(E, fns=fns):
                for f in fns:
                    f(E)

            getattr(block, eng)(body)
        self.streams = {e: [] for e in self.ENGS}


def build(nc, phases=4):
    dt = nc.dram_tensor
    x_in = dt("x_in", [T, D], F32, kind="ExternalInput")
    idn = dt("idn", [128, 128], F32, kind="ExternalInput")
    wqd = dt("wqd", [128, 4096], F32, kind="ExternalInput")
    wqk = dt("wqk", [128, 4096], F32, kind="ExternalInput")
    wkv = dt("wkv", [128, 512], F32, kind="ExternalInput")
    wvu = dt("wvu", [64, 1024], F32, kind="ExternalInput")
    wpj = dt("wpj", [128, 8192], F32, kind="ExternalInput")
    bqd = dt("bqd", [128, 4], F32, kind="ExternalInput")
    bqk = dt("bqk", [64, 16], F32, kind="ExternalInput")
    bkv = dt("bkv", [64, 1], F32, kind="ExternalInput")
    bvu = dt("bvu", [128, 8], F32, kind="ExternalInput")
    bpj = dt("bpj", [128, 8], F32, kind="ExternalInput")
    w1l = dt("w1l", [NL, NE, 16, 128, 2048], F32, kind="ExternalInput")
    w2l = dt("w2l", [NL, NE, 16, 128, 2048], F32, kind="ExternalInput")
    b1c = dt("b1c", [128, NL * NE * 32], F32, kind="ExternalInput")
    b2c = dt("b2c", [128, NL * NE * 8], F32, kind="ExternalInput")
    rwl = dt("rwl", [128, NL * 8 * 4], F32, kind="ExternalInput")
    rbr = dt("rbr", [1, NL * 4], F32, kind="ExternalInput")
    gcl = dt("gcl", [128, NL * NE * 8], F32, kind="ExternalInput")
    oh4 = dt("oh4", [4, 640], F32, kind="ExternalInput")
    bcl = dt("bcl", [128, NL * NE * 8], F32, kind="ExternalInput")
    out = dt("out", [T, D], F32, kind="ExternalOutput")
    _dbg = "ExternalOutput" if os.environ.get("K_DBG") else "Internal"
    xn_s = dt("xn_s", [128, 8 * T], F32, kind=_dbg)
    s_s = dt("s_s", [128, 8 * T], F32, kind=_dbg)
    m_s = dt("m_s", [16, T], F32, kind=_dbg)
    v_s = dt("v_s", [1, T], F32, kind=_dbg)

    es0 = contextlib.ExitStack()
    with es0:
        mksem = lambda n: Sem(es0.enter_context(nc.semaphore(n)))
        s_gd = mksem("s_gd")
        s_sd = mksem("s_sd")
        s_t = mksem("s_t")
        s_sc = mksem("s_sc")
        s_v = mksem("s_v")
        b = Bld()

        idn_s = es0.enter_context(nc.sbuf_tensor("idn_s", [128, 128], F32))
        ones1b = es0.enter_context(nc.sbuf_tensor("ones1b", [1, 128], BF16))
        ones1f = es0.enter_context(nc.sbuf_tensor("ones1f", [1, 128], F32))
        ones64b = es0.enter_context(nc.sbuf_tensor("ones64b", [64, 1], BF16))
        ones128b = es0.enter_context(nc.sbuf_tensor("ones128b", [128, 1], BF16))
        epsc = es0.enter_context(nc.sbuf_tensor("epsc", [128, 1], F32))

        def dma(eng, dst, src, sem=None):
            sem = sem or (s_gd if eng == "gpsimd" else s_sd)

            def f(E, dst=dst, src=src, sh=sem.h):
                E.dma_start(dst, src).then_inc(sh, 16)

            b.emit(eng, f)
            sem.n += 16
            return sem.n

        idn_done = dma("sync", idn_s[:], idn.ap())

        def init_memsets(E):
            E.memset(ones1b[:], 1.0)
            E.memset(ones1f[:], 1.0)
            E.memset(ones64b[:], 1.0)
            E.memset(ones128b[:], 1.0)
            E.memset(epsc[:], EPS).then_inc(s_v.h, 1)

        b.emit("vector", init_memsets)
        s_v.n += 1
        init_v = s_v.n

        es_ab = contextlib.ExitStack()
        es_xn = contextlib.ExitStack()

        xn_bf = es_xn.enter_context(nc.sbuf_tensor("xn_bf", [128, 8 * T], BF16, side="left"))
        with contextlib.ExitStack() as es1:
            sb1 = lambda n, s, d: es1.enter_context(nc.sbuf_tensor(n, s, d, side="right"))
            xt = sb1("xt", [128, 2 * 1024], F32)
            sqt = sb1("sqt", [128, 1024], F32)
            xnt = sb1("xnt", [128, 2 * 1024], F32)
            stg = sb1("stg", [128, 2 * 1024], F32)
            mucol = sb1("mucol", [128, 4], F32)
            vstg = sb1("vstg", [128, 32], F32)
            pstr = [es1.enter_context(nc.psum_tensor("pstr0", [128, 512], F32)),
                    es1.enter_context(nc.psum_tensor("pstr1", [128, 512], F32))]

            x_dma = {}
            xn_v = {}
            tr_ev = {}
            cast_sc = {}
            stg_dma = {}
            sq_used = {}
            _p1cut = int(os.environ.get("K_P1CUT", 99))
            for jt in range(int(os.environ.get("K_P1N", T // 128))):
                sl = jt % 2
                if jt >= 2:
                    b.wait("sync", s_v, xn_v[jt - 2])
                x_dma[jt] = dma("sync", xt[:, sl * 1024:(sl + 1) * 1024],
                                x_in.ap()[jt * 128:(jt + 1) * 128, :])
                if _p1cut < 2:
                    continue
                b.wait("scalar", s_sd, x_dma[jt])
                if jt >= 1:
                    b.wait("scalar", s_v, sq_used[jt - 1])

                def p1sq(E, sl=sl):
                    E.activation(sqt[:], xt[:, sl * 1024:(sl + 1) * 1024],
                                 AF.Square).then_inc(s_sc.h, 1)

                b.emit("scalar", p1sq)
                s_sc.n += 1
                if _p1cut < 3:
                    continue
                b.wait("vector", s_sc, s_sc.n)
                b.wait("vector", s_sd, x_dma[jt])
                if jt == 0:
                    b.wait("vector", s_v, init_v)

                def p1stats(E, sl=sl, jt=jt, base=s_v.n):
                    xs = xt[:, sl * 1024:(sl + 1) * 1024]
                    E.reduce_sum(mucol[:, 0:1], xs, axis=AX.X)
                    E.reduce_sum(mucol[:, 1:2], sqt[:], axis=AX.X).then_inc(s_v.h, 1)
                    E.wait_ge(s_v.h, base + 1)
                    E.tensor_scalar(mucol[:, 0:1], mucol[:, 0:1], 1.0 / D, None, OP.mult, OP.bypass)
                    E.tensor_scalar(mucol[:, 1:2], mucol[:, 1:2], 1.0 / D, None, OP.mult, OP.bypass).then_inc(s_v.h, 1)
                    E.wait_ge(s_v.h, base + 2)
                    E.tensor_scalar(mucol[:, 2:3], mucol[:, 0:1], mucol[:, 0:1], None, OP.mult, OP.bypass).then_inc(s_v.h, 1)
                    E.wait_ge(s_v.h, base + 3)
                    E.tensor_tensor(mucol[:, 1:2], mucol[:, 1:2], mucol[:, 2:3], OP.subtract).then_inc(s_v.h, 1)
                    E.wait_ge(s_v.h, base + 4)
                    E.tensor_copy(vstg[:, jt:jt + 1], mucol[:, 1:2]).then_inc(s_v.h, 1)

                b.emit("vector", p1stats)
                s_v.n += 5
                sq_used[jt] = s_v.n
                if _p1cut < 4:
                    continue
                b.wait("scalar", s_v, s_v.n)

                def p1rstd(E, base=s_sc.n):
                    E.activation(mucol[:, 2:3], mucol[:, 1:2], AF.Ln, bias=epsc[:]).then_inc(s_sc.h, 1)
                    E.wait_ge(s_sc.h, base + 1)
                    E.activation(mucol[:, 3:4], mucol[:, 2:3], AF.Exp, scale=-0.5).then_inc(s_sc.h, 1)

                b.emit("scalar", p1rstd)
                s_sc.n += 2
                if _p1cut < 5:
                    continue
                b.wait("vector", s_sc, s_sc.n)
                if jt >= 2:
                    b.wait("vector", s_t, tr_t[jt - 2])

                def p1xn(E, sl=sl):
                    E.tensor_scalar(xnt[:, sl * 1024:(sl + 1) * 1024],
                                    xt[:, sl * 1024:(sl + 1) * 1024],
                                    mucol[:, 0:1], mucol[:, 3:4], OP.subtract, OP.mult).then_inc(s_v.h, 1)

                b.emit("vector", p1xn)
                s_v.n += 1
                xn_v[jt] = s_v.n
                tr_t = tr_t if jt > 0 else {}
                if _p1cut < 6:
                    continue

                b.wait("tensor", s_v, xn_v[jt])
                if jt == 0:
                    b.wait("tensor", s_sd, idn_done)
                for half in range(2):
                    gi = jt * 2 + half
                    if gi >= 2:
                        b.wait("tensor", s_v, tr_ev[gi - 2])

                    def p1tr(E, sl=sl, half=half):
                        last = None
                        for k in range(4):
                            kd = half * 4 + k
                            last = E.transpose(pstr[half][:, k * 128:(k + 1) * 128],
                                               xnt[:, sl * 1024 + kd * 128: sl * 1024 + (kd + 1) * 128],
                                               idn_s[:])
                        last.then_inc(s_t.h, 1)

                    b.emit("tensor", p1tr)
                    s_t.n += 1
                    b.wait("vector", s_t, s_t.n)
                    if gi >= 4:
                        b.wait("vector", s_sc, cast_sc.get(jt - 2, 0))
                        b.wait("vector", s_sd, stg_dma.get(jt - 2, 0))

                    def p1ev(E, sl=sl, half=half):
                        E.tensor_copy(stg[:, sl * 1024 + half * 512: sl * 1024 + (half + 1) * 512],
                                      pstr[half][:]).then_inc(s_v.h, 1)

                    b.emit("vector", p1ev)
                    s_v.n += 1
                    tr_ev[gi] = s_v.n
                tr_t[jt] = s_t.n
                if _p1cut < 7:
                    continue
                b.wait("scalar", s_v, tr_ev[jt * 2 + 1])

                def p1cast(E, sl=sl, jt=jt):
                    E.activation(
                        bass.AP(xn_bf, jt * 128, [[8 * T, 128], [T, 8], [1, 128]]),
                        stg.ap()[:, sl * 1024:(sl + 1) * 1024].rearrange("p (c f) -> p c f", c=8),
                        AF.Copy).then_inc(s_sc.h, 1)

                b.emit("scalar", p1cast)
                s_sc.n += 1
                cast_sc[jt] = s_sc.n
                if _p1cut < 8:
                    continue
                b.wait("sync", s_v, tr_ev[jt * 2 + 1])
                stg_dma[jt] = dma(
                    "sync", bass.AP(xn_s, jt * 128, [[8 * T, 128], [T, 8], [1, 128]]),
                    stg.ap()[:, sl * 1024:(sl + 1) * 1024].rearrange("p (c f) -> p c f", c=8))
            p1_cast_done = s_sc.n
            b.wait("sync", s_v, s_v.n)
            for jt in range(T // 128):
                dma("sync", bass.AP(v_s, jt * 128, [[1, 1], [1, 128]]),
                    vstg[:, jt:jt + 1])
            b.barrier([s_t, s_sc, s_v, s_gd, s_sd])


        def _checkpoint():
            b.wait("gpsimd", s_gd, s_gd.n)
            b.wait("sync", s_sd, s_sd.n)
            b.barrier([s_t, s_sc, s_v])
            blk_ = es0.enter_context(nc.Block())
            b.run(blk_)
        if phases < 2:
            _checkpoint()
            es_xn.close()
            return nc

        sab = lambda n, s, d: es_ab.enter_context(nc.sbuf_tensor(n, s, d, side="right"))
        ckv = sab("ckv", [65, T], BF16)
        wvu_s = sab("wvu_s", [64, 1024], BF16)
        wpj_s = sab("wpj_s", [128, 8192], BF16)
        btot = sab("btot", [128, 8], F32)
        kn2m = sab("kn2m", [1, 1], F32)

        with contextlib.ExitStack() as es2a:
            sb2 = lambda n, s, d: es2a.enter_context(nc.sbuf_tensor(n, s, d, side="right"))
            cq = sb2("cq", [128, 4 * T], BF16)
            wqd_s = sb2("wqd_s", [128, 4096], BF16)
            wkv_s = sb2("wkv_s", [128, 512], BF16)
            wqk_s = sb2("wqk_s", [128, 4096], BF16)
            bqd_s = sb2("bqd_s", [128, 4], F32)
            bkv_s = sb2("bkv_s", [64, 1], F32)
            bqk_s = sb2("bqk_s", [64, 16], F32)
            bvu_s = sb2("bvu_s", [128, 8], F32)
            bvu_b = sb2("bvu_b", [128, 8], BF16)
            bpj_s = sb2("bpj_s", [128, 8], F32)
            mstg = sb2("mstg", [1, 2 * T], F32)
            psq = [es2a.enter_context(nc.psum_tensor("psq0", [128, 512], F32)),
                   es2a.enter_context(nc.psum_tensor("psq1", [128, 512], F32))]
            psn = es2a.enter_context(nc.psum_tensor("psn", [128, 1024], F32))

            wq_done = dma("gpsimd", wqd_s[:], wqd.ap())
            wkv_done = dma("gpsimd", wkv_s[:], wkv.ap())
            wqk_done = dma("gpsimd", wqk_s[:], wqk.ap())
            wvu_done = dma("gpsimd", wvu_s[:], wvu.ap())
            wpj_done = dma("gpsimd", wpj_s[:], wpj.ap())
            dma("sync", bqd_s[:], bqd.ap())
            dma("sync", bkv_s[:], bkv.ap())
            dma("sync", bqk_s[:], bqk.ap())
            dma("sync", bvu_s[:], bvu.ap())
            bias_done = dma("sync", bpj_s[:], bpj.ap())

            b.wait("vector", s_sd, bias_done)

            def p2bias(E):
                E.tensor_scalar(bqk_s[:], bqk_s[:], 0.125, None, OP.mult, OP.bypass)
                E.tensor_copy(bvu_b[:], bvu_s[:])
                E.memset(ckv[64:65, :], -1.0).then_inc(s_v.h, 1)

            b.emit("vector", p2bias)
            s_v.n += 1
            bias_v = s_v.n

            b.wait("tensor", s_gd, wkv_done)
            b.wait("tensor", s_sc, p1_cast_done)
            ckv_sc = {}
            for tq in range(8):
                if tq >= 2:
                    b.wait("tensor", s_sc, ckv_sc[tq - 2])

                def ckvmm(E, tq=tq):
                    last = None
                    for kd in range(8):
                        last = E.matmul(psq[tq % 2][0:64, :],
                                        wkv_s[:, kd * 64:(kd + 1) * 64],
                                        xn_bf[:, kd * T + tq * 512: kd * T + (tq + 1) * 512],
                                        start=(kd == 0), stop=(kd == 7))
                    last.then_inc(s_t.h, 1)

                b.emit("tensor", ckvmm)
                s_t.n += 1
                b.wait("scalar", s_t, s_t.n)
                if tq == 0:
                    b.wait("scalar", s_sd, bias_done)

                def ckvev(E, tq=tq):
                    E.activation(ckv[0:64, tq * 512:(tq + 1) * 512], psq[tq % 2][0:64, :],
                                 AF.Identity, bias=bkv_s[:]).then_inc(s_sc.h, 1)

                b.emit("scalar", ckvev)
                s_sc.n += 1
                ckv_sc[tq] = s_sc.n
            ckv_done = s_sc.n

            b.wait("tensor", s_gd, wq_done)
            cq_sc = {}
            for g in range(32):
                mq, tq = g // 8, g % 8
                if g >= 2:
                    b.wait("tensor", s_sc, cq_sc[g - 2])
                elif g == 0:
                    b.wait("tensor", s_sc, ckv_done)

                def cqmm(E, mq=mq, tq=tq, g=g):
                    last = None
                    for kd in range(8):
                        last = E.matmul(psq[g % 2][:],
                                        wqd_s[:, (kd * 4 + mq) * 128:(kd * 4 + mq + 1) * 128],
                                        xn_bf[:, kd * T + tq * 512: kd * T + (tq + 1) * 512],
                                        start=(kd == 0), stop=(kd == 7))
                    last.then_inc(s_t.h, 1)

                b.emit("tensor", cqmm)
                s_t.n += 1
                b.wait("scalar", s_t, s_t.n)

                def cqev(E, mq=mq, tq=tq, g=g):
                    E.activation(cq[:, mq * T + tq * 512: mq * T + (tq + 1) * 512],
                                 psq[g % 2][:], AF.Identity,
                                 bias=bqd_s[:, mq:mq + 1]).then_inc(s_sc.h, 1)

                b.emit("scalar", cqev)
                s_sc.n += 1
                cq_sc[g] = s_sc.n
            cq_done = s_sc.n

            es_xn.close()
            q_all = es_ab.enter_context(nc.sbuf_tensor("q_all", [128, 8 * T], BF16, side="left"))
            ckvsq = sb2("ckvsq", [64, T], BF16)
            kn2r = sb2("kn2r", [1, T], F32)
            sqst = sb2("sqst", [64, 2 * 512], BF16)

            b.wait("scalar", s_sc, ckv_done)

            def ksq(E):
                E.activation(ckvsq[:], ckv[0:64, :], AF.Square).then_inc(s_sc.h, 1)

            b.emit("scalar", ksq)
            s_sc.n += 1
            b.wait("tensor", s_sc, s_sc.n)
            b.wait("tensor", s_v, init_v)
            kn_v = {}
            for tq in range(8):
                if tq > 0:
                    b.wait("tensor", s_v, kn_v[tq - 1])

                def knmm(E, tq=tq):
                    E.matmul(psn[0:1, 0:512], ones64b[:],
                             ckvsq[:, tq * 512:(tq + 1) * 512],
                             start=True, stop=True).then_inc(s_t.h, 1)

                b.emit("tensor", knmm)
                s_t.n += 1
                b.wait("vector", s_t, s_t.n)

                def knev(E, tq=tq):
                    E.tensor_copy(kn2r[:, tq * 512:(tq + 1) * 512],
                                  psn[0:1, 0:512]).then_inc(s_v.h, 1)

                b.emit("vector", knev)
                s_v.n += 1
                kn_v[tq] = s_v.n

            def knmax(E):
                E.reduce_max(kn2m[:], kn2r[:], axis=AX.X).then_inc(s_v.h, 1)

            b.emit("vector", knmax)
            s_v.n += 1
            kn2m_v = s_v.n

            b.wait("tensor", s_gd, wqk_done)
            qg_sc = {}
            for g in range(H * 8):
                h, tq = g // 8, g % 8
                r0, ch = 64 * (h % 2), h // 2
                if g >= 2:
                    b.wait("tensor", s_sc, qg_sc[g - 2])
                elif g == 0:
                    b.wait("tensor", s_sc, cq_done)

                def qmm(E, h=h, tq=tq, g=g):
                    last = None
                    for kc in range(4):
                        last = E.matmul(psq[g % 2][0:64, :],
                                        wqk_s[:, (kc * 16 + h) * 64:(kc * 16 + h + 1) * 64],
                                        cq[:, kc * T + tq * 512: kc * T + (tq + 1) * 512],
                                        start=(kc == 0), stop=(kc == 3))
                    last.then_inc(s_t.h, 1)

                b.emit("tensor", qmm)
                s_t.n += 1
                b.wait("scalar", s_t, s_t.n)
                if g == 0:
                    b.wait("scalar", s_v, bias_v)

                def qev(E, h=h, tq=tq, g=g, r0=r0, ch=ch):
                    E.activation(q_all[r0:r0 + 64, ch * T + tq * 512: ch * T + (tq + 1) * 512],
                                 psq[g % 2][0:64, :], AF.Identity, scale=0.125,
                                 bias=bqk_s[:, h:h + 1]).then_inc(s_sc.h, 1)

                b.emit("scalar", qev)
                s_sc.n += 1
                qg_sc[g] = s_sc.n

            qn_t = {}
            m_sc = {}
            m_dma = {}
            for g in range(H * 8):
                h, tq = g // 8, g % 8
                r0, ch = 64 * (h % 2), h // 2
                if g >= 2:
                    b.wait("scalar", s_t, qn_t[g - 2])
                if g == 0:
                    b.wait("scalar", s_sc, qg_sc[H * 8 - 1])

                def qsq(E, g=g, r0=r0, ch=ch, tq=tq):
                    E.activation(sqst[:, (g % 2) * 512:(g % 2 + 1) * 512],
                                 q_all[r0:r0 + 64, ch * T + tq * 512: ch * T + (tq + 1) * 512],
                                 AF.Square).then_inc(s_sc.h, 1)

                b.emit("scalar", qsq)
                s_sc.n += 1
                b.wait("tensor", s_sc, s_sc.n)
                if g == 0:
                    b.wait("tensor", s_v, kn2m_v)
                if g > 0:
                    b.wait("tensor", s_sc, m_sc[g - 1])

                def qnmm(E, g=g):
                    E.matmul(psn[0:1, 512:1024], ones64b[:],
                             sqst[:, (g % 2) * 512:(g % 2 + 1) * 512],
                             start=True, stop=True).then_inc(s_t.h, 1)

                b.emit("tensor", qnmm)
                s_t.n += 1
                qn_t[g] = s_t.n
                b.wait("scalar", s_t, s_t.n)
                if g >= 16:
                    b.wait("scalar", s_sd, m_dma.get(h - 1, 0))

                def mev(E, h=h, tq=tq):
                    E.activation(mstg[:, (h % 2) * T + tq * 512:(h % 2) * T + (tq + 1) * 512],
                                 psn[0:1, 512:1024], AF.Sqrt, scale=kn2m[:]).then_inc(s_sc.h, 1)

                b.emit("scalar", mev)
                s_sc.n += 1
                m_sc[g] = s_sc.n
                if tq == 7:
                    b.wait("sync", s_sc, s_sc.n)
                    m_dma[h] = dma("sync", m_s.ap()[h:h + 1, :],
                                   mstg[:, (h % 2) * T:(h % 2 + 1) * T])

            b.wait("tensor", s_gd, wpj_done)
            b.wait("tensor", s_v, bias_v)
            b.wait("tensor", s_sc, m_sc[H * 8 - 1])

            def btmm(E):
                last = None
                for mc in range(8):
                    for kv in range(8):
                        last = E.matmul(psn[:, mc:mc + 1],
                                        wpj_s[:, (kv * 8 + mc) * 128:(kv * 8 + mc + 1) * 128],
                                        bvu_b[:, kv:kv + 1],
                                        start=(kv == 0), stop=(kv == 7))
                last.then_inc(s_t.h, 1)

            b.emit("tensor", btmm)
            s_t.n += 1
            b.wait("vector", s_t, s_t.n)

            def btev(E, base=s_v.n):
                E.tensor_copy(btot[:], psn[:, 0:8]).then_inc(s_v.h, 1)
                E.wait_ge(s_v.h, base + 1)
                E.tensor_tensor(btot[:], btot[:], bpj_s[:], OP.add).then_inc(s_v.h, 1)

            b.emit("vector", btev)
            s_v.n += 2
            b.barrier([s_t, s_sc, s_v, s_gd, s_sd])

        if phases < 3:
            _checkpoint()
            es_ab.close()
            return nc

        with contextlib.ExitStack() as es2b:
            sb2 = lambda n, s, d: es2b.enter_context(nc.sbuf_tensor(n, s, d, side="right"))
            y_fm = sb2("y_fm", [128, 8 * T], BF16)
            qa = [sb2("qa0", [65, T], BF16), sb2("qa1", [65, T], BF16)]
            vh = [sb2("vh0", [128, 32 * 65], BF16), sb2("vh1", [128, 32 * 65], BF16)]
            pt = [sb2("pt0", [128, 1024], BF16), sb2("pt1", [128, 1024], BF16)]
            rrow = sb2("rrow", [1, 1024], F32)
            rbsb = sb2("rbsb", [64, 1024], F32)
            sstg = sb2("sstg", [128, 2 * 512], F32)
            psS = [es2b.enter_context(nc.psum_tensor("psS0", [128, 1024], F32)),
                   es2b.enter_context(nc.psum_tensor("psS1", [128, 1024], F32))]
            psY = [es2b.enter_context(nc.psum_tensor("psY0", [65, 1024], F32)),
                   es2b.enter_context(nc.psum_tensor("psY1", [65, 1024], F32))]

            def vones(E):
                E.memset(bass.AP(vh[0], 64, [[32 * 65, 128], [65, 32]]), 1.0)
                E.memset(bass.AP(vh[1], 64, [[32 * 65, 128], [65, 32]]), 1.0).then_inc(s_v.h, 1)

            b.emit("vector", vones)
            s_v.n += 1

            qa_q_v = {}
            qa_m_d = {}
            head_t = {}
            vh_v = {}
            last_ytt = {}
            exp_last = {}
            rb_ev = 0

            for h in range(H):
                r0, ch, pp = 64 * (h % 2), h // 2, h % 2
                if h >= 2:
                    b.wait("vector", s_t, head_t[h - 2])
                    b.wait("gpsimd", s_t, head_t[h - 2])

                def qbuild(E, r0=r0, ch=ch, pp=pp):
                    E.tensor_copy(qa[pp][0:64, :],
                                  q_all[r0:r0 + 64, ch * T:(ch + 1) * T]).then_inc(s_v.h, 1)

                b.emit("vector", qbuild)
                s_v.n += 1
                qa_q_v[h] = s_v.n
                qa_m_d[h] = dma("gpsimd", qa[pp][64:65, :], m_s.ap()[h:h + 1, :])

                if h > 0:
                    b.wait("tensor", s_sc, exp_last[h - 1])
                    b.wait("tensor", s_v, rb_ev)
                for half in range(2):
                    def vmm(E, h=h, half=half):
                        last = None
                        for k in range(16):
                            kc = half * 16 + k
                            last = E.matmul(psS[half][:, k * 64:(k + 1) * 64],
                                            ckv[0:64, kc * 128:(kc + 1) * 128],
                                            wvu_s[:, h * 64:(h + 1) * 64],
                                            start=True, stop=True)
                        last.then_inc(s_t.h, 1)

                    b.emit("tensor", vmm)
                    s_t.n += 1
                    b.wait("vector", s_t, s_t.n)
                    if h >= 2:
                        b.wait("vector", s_t, head_t[h - 2])

                    def vhev(E, pp=pp, half=half):
                        E.tensor_copy(
                            bass.AP(vh[pp], half * 16 * 65, [[32 * 65, 128], [65, 16], [1, 64]]),
                            psS[half].ap().rearrange("p (c f) -> p c f", c=16)).then_inc(s_v.h, 1)

                    b.emit("vector", vhev)
                    s_v.n += 1
                vh_v[h] = s_v.n

                b.wait("tensor", s_v, qa_q_v[h])
                b.wait("tensor", s_gd, qa_m_d[h])
                b.wait("tensor", s_v, vh_v[h])
                if h == 0:
                    b.wait("scalar", s_sc, p2a_done_sc) if False else None
                exp_sc = {}
                pv_t = {}

                def emit_S(qB, k):
                    gk = qB * 32 + k
                    kc = k

                    def smm(E, pp=pp, qB=qB, kc=kc):
                        E.matmul(psS[kc % 2][:, 0:512], ckv[:, kc * 128:(kc + 1) * 128],
                                 qa[pp][:, qB * 1024:qB * 1024 + 512], start=True, stop=True)
                        E.matmul(psS[kc % 2][:, 512:1024], ckv[:, kc * 128:(kc + 1) * 128],
                                 qa[pp][:, qB * 1024 + 512:(qB + 1) * 1024],
                                 start=True, stop=True).then_inc(s_t.h, 1)

                    b.emit("tensor", smm)
                    s_t.n += 1
                    b.wait("scalar", s_t, s_t.n)
                    if gk >= 2:
                        b.wait("scalar", s_t, pv_t[gk - 2])

                    def expv(E, kc=kc):
                        E.activation(pt[kc % 2][:], psS[kc % 2][:], AF.Exp).then_inc(s_sc.h, 1)

                    b.emit("scalar", expv)
                    s_sc.n += 1
                    exp_sc[gk] = s_sc.n

                emit_S(0, 0)
                if h > 0:
                    b.wait("tensor", s_v, rb_ev)
                emit_S(0, 1)
                for qB in range(4):
                    for k in range(32):
                        gk = qB * 32 + k
                        b.wait("tensor", s_sc, exp_sc[gk])
                        if k == 0:
                            b.wait("tensor", s_v, last_ytt.get(qB % 2, 0))

                        def pvmm(E, pp=pp, kc=k, qB=qB):
                            py = psY[qB % 2]
                            E.matmul(py[:, 0:512], vh[pp][:, kc * 65:(kc + 1) * 65],
                                     pt[kc % 2][:, 0:512], start=(kc == 0), stop=(kc == 31))
                            E.matmul(py[:, 512:1024], vh[pp][:, kc * 65:(kc + 1) * 65],
                                     pt[kc % 2][:, 512:1024],
                                     start=(kc == 0), stop=(kc == 31)).then_inc(s_t.h, 1)

                        b.emit("tensor", pvmm)
                        s_t.n += 1
                        pv_t[gk] = s_t.n
                        if k + 2 < 32:
                            emit_S(qB, k + 2)
                    b.wait("scalar", s_t, pv_t[qB * 32 + 31])

                    def rrowv(E, qB=qB, base=s_sc.n):
                        E.activation(rrow[:], psY[qB % 2][64:65, :], AF.Ln).then_inc(s_sc.h, 1)
                        E.wait_ge(s_sc.h, base + 1)
                        E.activation(rrow[:], rrow[:], AF.Exp, scale=-1.0).then_inc(s_sc.h, 1)

                    b.emit("scalar", rrowv)
                    s_sc.n += 2
                    rrow_sc = s_sc.n
                    if qB < 3:
                        emit_S(qB + 1, 0)
                    b.wait("tensor", s_sc, rrow_sc)

                    def rbmm(E):
                        E.matmul(psS[1][0:64, 0:512], ones1f[:, 0:64], rrow[:, 0:512],
                                 start=True, stop=True)
                        E.matmul(psS[1][0:64, 512:1024], ones1f[:, 0:64], rrow[:, 512:1024],
                                 start=True, stop=True).then_inc(s_t.h, 1)

                    b.emit("tensor", rbmm)
                    s_t.n += 1
                    b.wait("vector", s_t, s_t.n)

                    def rbev(E, r0=r0, ch=ch, qB=qB, base=s_v.n):
                        E.tensor_copy(rbsb[:], psS[1][0:64, :]).then_inc(s_v.h, 1)
                        E.wait_ge(s_v.h, base + 1)
                        E.tensor_tensor(
                            y_fm[r0:r0 + 64, ch * T + qB * 1024: ch * T + (qB + 1) * 1024],
                            psY[qB % 2][0:64, :], rbsb[:], OP.mult).then_inc(s_v.h, 1)

                    b.emit("vector", rbev)
                    s_v.n += 2
                    rb_ev = s_v.n - 1
                    last_ytt[qB % 2] = s_v.n
                    if qB < 3:
                        b.wait("tensor", s_v, rb_ev)
                        emit_S(qB + 1, 1)
                exp_last[h] = s_sc.n
                head_t[h] = s_t.n
            exp_sc_last = exp_last[H - 1]

            pj_sc = {}
            pj_d = {}
            for g in range(64):
                tq, mc = g // 8, g % 8
                if g == 0:
                    b.wait("tensor", s_v, last_ytt.get(0, 0))
                    b.wait("tensor", s_v, last_ytt.get(1, 0))
                    b.wait("tensor", s_v, rb_ev)
                    b.wait("tensor", s_sc, exp_sc_last)
                if g >= 2:
                    b.wait("tensor", s_sc, pj_sc[g - 2])

                def pjmm(E, tq=tq, mc=mc, g=g):
                    last = None
                    for kv in range(8):
                        last = E.matmul(psS[g % 2][:, 0:512],
                                        wpj_s[:, (kv * 8 + mc) * 128:(kv * 8 + mc + 1) * 128],
                                        y_fm[:, kv * T + tq * 512: kv * T + (tq + 1) * 512],
                                        start=(kv == 0), stop=(kv == 7))
                    last.then_inc(s_t.h, 1)

                b.emit("tensor", pjmm)
                s_t.n += 1
                b.wait("scalar", s_t, s_t.n)
                if g >= 2:
                    b.wait("scalar", s_sd, pj_d[g - 2])

                def pjev(E, mc=mc, g=g):
                    E.activation(sstg[:, (g % 2) * 512:(g % 2 + 1) * 512],
                                 psS[g % 2][:, 0:512], AF.Identity,
                                 bias=btot[:, mc:mc + 1]).then_inc(s_sc.h, 1)

                b.emit("scalar", pjev)
                s_sc.n += 1
                pj_sc[g] = s_sc.n
                b.wait("sync", s_sc, s_sc.n)
                pj_d[g] = dma("sync", s_s.ap()[:, mc * T + tq * 512: mc * T + (tq + 1) * 512],
                              sstg[:, (g % 2) * 512:(g % 2 + 1) * 512])
            b.barrier([s_t, s_sc, s_v, s_gd, s_sd])
        es_ab.close()

        if phases < 4:
            _checkpoint()
            return nc

        with contextlib.ExitStack() as es3:
            sb3 = lambda n, s, d: es3.enter_context(nc.sbuf_tensor(n, s, d, side="right"))
            hten = sb3("hten", [128, 32 * TCH], BF16)
            x_c = sb3("x_c", [128, 8 * TCH], F32)
            xb = sb3("xb", [128, 8 * TCH], BF16)
            xhat = sb3("xhat", [128, 8 * TCH], BF16)
            w1s = sb3("w1s", [128, 2 * 2048], BF16)
            w2s = sb3("w2s", [128, 2 * 2048], BF16)
            b1s = sb3("b1s", [128, NL * NE * 32], F32)
            b2s = sb3("b2s", [128, NL * NE * 8], F32)
            rws = sb3("rws", [128, NL * 8 * 4], F32)
            rbs = sb3("rbs", [1, NL * 4], F32)
            rbsbf = sb3("rbsbf", [1, NL * 4], BF16)
            gcs = sb3("gcs", [128, NL * NE * 8], F32)
            bcs = sb3("bcs", [128, NL * NE * 8], F32)
            oh4s = sb3("oh4s", [4, 640], F32)
            sqtmp = sb3("sqtmp", [128, 8 * TCH], BF16)
            lgt = sb3("lgt", [128, 16], F32)
            comb5 = sb3("comb5", [128, 8], F32)
            comb_all = sb3("comb_all", [128, 32], F32)
            c4 = sb3("c4", [4, TCH], F32)
            rowA = sb3("rowA", [1, TCH], F32)
            rowB = sb3("rowB", [1, TCH], F32)
            rowC = sb3("rowC", [1, TCH], F32)
            rbb_all = sb3("rbb_all", [128, 16], F32)
            cb_b = sb3("cb_b", [128, TCH], F32)
            tmpv = sb3("tmpv", [128, 2 * 512], F32)
            sti = sb3("sti", [128, TCH], F32)
            gbuf = sb3("gbuf", [128, TCH], F32)
            outsb = sb3("outsb", [128, 1024], F32)
            psh = [es3.enter_context(nc.psum_tensor("psh0", [128, 1024], F32)),
                   es3.enter_context(nc.psum_tensor("psh1", [128, 1024], F32))]
            psy = [es3.enter_context(nc.psum_tensor("psy0", [128, 512], F32)),
                   es3.enter_context(nc.psum_tensor("psy1", [128, 512], F32))]
            psm = es3.enter_context(nc.psum_tensor("psm", [128, 1024], F32))

            for d_, s_ in ((b1s, b1c), (b2s, b2c), (rws, rwl), (rbs, rbr),
                           (gcs, gcl), (bcs, bcl), (oh4s, oh4)):
                static_ffn = dma("gpsimd", d_[:], s_.ap())
            b.wait("vector", s_gd, static_ffn)
            b.emit("vector", lambda E: E.tensor_copy(rbsbf[:], rbs[:]).then_inc(s_v.h, 1))
            s_v.n += 1
            static_v = s_v.n

            b.wait("tensor", s_v, static_v)

            def rbbmm(E):
                E.matmul(psm[:, 0:16], ones1b[:], rbsbf[:, 0:16],
                         start=True, stop=True).then_inc(s_t.h, 1)

            b.emit("tensor", rbbmm)
            s_t.n += 1
            b.wait("vector", s_t, s_t.n)

            def rbbev(E):
                E.tensor_copy(rbb_all[:], psm[:, 0:16]).then_inc(s_v.h, 1)

            b.emit("vector", rbbev)
            s_v.n += 1
            psm_free_v = s_v.n

            wstart = s_gd.n
            nblk = NCH * NL * NE * 32
            wblk_n = [wstart + 16 * (g + 1) for g in range(nblk)]
            s_gd.n = wstart + 16 * nblk
            blk_done = {}
            combine_done_v = {}

            def emit_weight_dmas(c, l, e):
                eg = (c * NL + l) * NE + e
                for i in range(32):
                    fam = 0 if i < 16 else 1
                    fi = eg * 16 + (i if i < 16 else i - 16)
                    pg = -1
                    if fi >= 2:
                        pg = (fi - 2) // 16 * 32 + (0 if fam == 0 else 16) + (fi - 2) % 16
                    guard = blk_done[pg] if pg >= 0 else 0
                    vguard = combine_done_v.get(c - 1, 0) if (l == 0 and e == 0) else 0
                    if fam == 0:
                        srcap = w1l.ap()[l, e, i, :, :]
                        dst = w1s[:, (i % 2) * 2048:(i % 2 + 1) * 2048]
                    else:
                        srcap = w2l.ap()[l, e, i - 16, :, :]
                        dst = w2s[:, (i % 2) * 2048:(i % 2 + 1) * 2048]

                    def wdma(E, d=dst, s=srcap, g=guard, vg=vguard):
                        if g:
                            E.wait_ge(s_t.h, g)
                        if vg:
                            E.wait_ge(s_v.h, vg)
                        E.dma_start(d, s).then_inc(s_gd.h, 16)

                    b.emit("gpsimd", wdma)

            h_evac = {}
            y_evac = {}
            hlp = {"sx_last_t": 0, "rowev_v": 0, "wrow_v": 0}

            def emit_sxk(kd, last):
                if kd == 0:
                    b.wait("tensor", s_v, psm_free_v)
                b.wait("tensor", s_v, sq_v[kd])

                def sxk(E, kd=kd, last=last):
                    lm = None
                    for tsc in range(2):
                        E.matmul(psm[0:1, tsc * 512:(tsc + 1) * 512], ones128b[:],
                                 xb[:, kd * TCH + tsc * 512: kd * TCH + (tsc + 1) * 512],
                                 start=(kd == 0), stop=last)
                        lm = E.matmul(psm[64:65, tsc * 512:(tsc + 1) * 512], ones128b[:],
                                      sqtmp[:, kd * TCH + tsc * 512: kd * TCH + (tsc + 1) * 512],
                                      start=(kd == 0), stop=last)
                    lm.then_inc(s_t.h, 1)

                b.emit("tensor", sxk)
                s_t.n += 1
                if last:
                    hlp["sx_last_t"] = s_t.n

            def emit_rows():
                b.wait("vector", s_t, hlp["sx_last_t"])

                def rowev(E, base=s_v.n):
                    E.tensor_copy(rowA[:], psm[0:1, 0:1024]).then_inc(s_v.h, 1)
                    E.wait_ge(s_v.h, base + 1)
                    E.tensor_copy(rowB[:], psm[64:65, 0:1024]).then_inc(s_v.h, 1)

                b.emit("vector", rowev)
                s_v.n += 2
                hlp["rowev_v"] = s_v.n

                def murow(E, base=s_v.n):
                    E.tensor_scalar(rowA[:], rowA[:], 1.0 / D, None, OP.mult, OP.bypass)
                    E.tensor_scalar(rowB[:], rowB[:], 1.0 / D, None, OP.mult, OP.bypass).then_inc(s_v.h, 1)
                    E.wait_ge(s_v.h, base + 1)
                    E.tensor_tensor(rowC[:], rowA[:], rowA[:], OP.mult).then_inc(s_v.h, 1)
                    E.wait_ge(s_v.h, base + 2)
                    E.tensor_tensor(rowB[:], rowB[:], rowC[:], OP.subtract).then_inc(s_v.h, 1)

                b.emit("vector", murow)
                s_v.n += 3
                b.wait("scalar", s_v, s_v.n)

                def rstdrow(E, base=s_sc.n):
                    E.activation(rowC[:], rowB[:], AF.Ln, bias=epsc[0:1, :]).then_inc(s_sc.h, 1)
                    E.wait_ge(s_sc.h, base + 1)
                    E.activation(rowC[:], rowC[:], AF.Exp, scale=-0.5).then_inc(s_sc.h, 1)

                b.emit("scalar", rstdrow)
                s_sc.n += 2
                b.wait("vector", s_sc, s_sc.n)

                def wrowv(E):
                    E.tensor_tensor(rowB[:], rowA[:], rowC[:], OP.mult).then_inc(s_v.h, 1)

                b.emit("vector", wrowv)
                s_v.n += 1
                hlp["wrow_v"] = s_v.n

            def emit_sums_rows():
                for kd in range(8):
                    emit_sxk(kd, kd == 7)
                emit_rows()

            prtr_last_v = 0

            def emit_l0_rows(cc, war_t, war_v):
                if war_v:
                    b.wait("sync", s_v, war_v)
                if war_t:
                    b.wait("sync", s_t, war_t)
                vd = dma("sync", rowA[:], bass.AP(v_s, cc * TCH, [[1, 1], [1, TCH]]))
                b.wait("vector", s_sd, vd)

                def l0va(E, base=s_v.n):
                    E.tensor_scalar(rowB[:], rowA[:], EPS, None, OP.add, OP.bypass).then_inc(s_v.h, 1)
                    E.wait_ge(s_v.h, base + 1)
                    E.reciprocal(rowB[:], rowB[:]).then_inc(s_v.h, 1)
                    E.wait_ge(s_v.h, base + 2)
                    E.tensor_tensor(rowB[:], rowA[:], rowB[:], OP.mult).then_inc(s_v.h, 1)

                b.emit("vector", l0va)
                s_v.n += 3
                b.wait("scalar", s_v, s_v.n)

                def l0rstd(E, base=s_sc.n):
                    E.activation(rowC[:], rowB[:], AF.Ln, bias=epsc[0:1, :]).then_inc(s_sc.h, 1)
                    E.wait_ge(s_sc.h, base + 1)
                    E.activation(rowC[:], rowC[:], AF.Exp, scale=-0.5).then_inc(s_sc.h, 1)

                b.emit("scalar", l0rstd)
                s_sc.n += 2
                b.wait("vector", s_sc, s_sc.n)

                def l0wz(E):
                    E.memset(rowB[:], 0.0).then_inc(s_v.h, 1)

                b.emit("vector", l0wz)
                s_v.n += 1
                hlp["wrow_v"] = s_v.n
                hlp["rowev_v"] = s_v.n
            out_dma_prev = 0
            xb_free_t = 0
            cb_ev_last = 0
            cb_t_last = 0
            sq_v = {}
            pend_stats = False
            pend_sums = False

            emit_l0_rows(0, 0, 0)
            pend_sums = True
            xc_done = dma("sync", x_c.ap().rearrange("p (k t) -> p k t", k=8),
                          bass.AP(xn_s, 0, [[8 * T, 128], [T, 8], [1, TCH]]))
            for c in range(NCH):
                s_dma = {}
                prod_v = {}
                for l in range(NL):
                    li = c * NL + l
                    if not pend_stats and not pend_sums:
                        b.wait("vector", s_sd, xc_done)
                        b.wait("vector", s_t, xb_free_t)
                        for kd in range(8):
                            def xbsq(E, kd=kd, base=s_v.n):
                                E.tensor_copy(xb[:, kd * TCH:(kd + 1) * TCH],
                                              x_c[:, kd * TCH:(kd + 1) * TCH]).then_inc(s_v.h, 1)
                                E.wait_ge(s_v.h, base + 1)
                                E.tensor_tensor(sqtmp[:, kd * TCH:(kd + 1) * TCH],
                                                xb[:, kd * TCH:(kd + 1) * TCH],
                                                xb[:, kd * TCH:(kd + 1) * TCH],
                                                OP.mult).then_inc(s_v.h, 1)

                            b.emit("vector", xbsq)
                            s_v.n += 2
                            sq_v[kd] = s_v.n
                    pend_stats = False

                    if not pend_sums:
                        emit_sums_rows()
                    pend_sums = False

                    b.wait("tensor", s_v, hlp["wrow_v"])
                    if li > 0:
                        b.wait("tensor", s_sc, h_evac[li * NE * 32 - 1])
                    if c > 0 and l == 0:
                        b.wait("tensor", s_v, prtr_last_v)
                    for rowt, dstp in ((rowC, psh[0]), (rowB, psh[1])):
                        def bcmm(E, rowt=rowt, dstp=dstp):
                            E.matmul(dstp[:, 0:512], ones1f[:], rowt[:, 0:512], start=True, stop=True)
                            E.matmul(dstp[:, 512:1024], ones1f[:], rowt[:, 512:1024],
                                     start=True, stop=True).then_inc(s_t.h, 1)

                        b.emit("tensor", bcmm)
                        s_t.n += 1
                    bc_t = s_t.n

                    b.wait("vector", s_t, bc_t)
                    if l == 0:
                        b.wait("vector", s_sd, xc_done)
                    for kd in range(8):
                        def xhatk(E, kd=kd, base=s_v.n):
                            E.tensor_tensor(tmpv[:, 0:512],
                                            x_c[:, kd * TCH: kd * TCH + 512], psh[0][:, 0:512], OP.mult)
                            E.tensor_tensor(tmpv[:, 512:1024],
                                            x_c[:, kd * TCH + 512:(kd + 1) * TCH],
                                            psh[0][:, 512:1024], OP.mult).then_inc(s_v.h, 1)
                            E.wait_ge(s_v.h, base + 1)
                            E.tensor_tensor(xhat[:, kd * TCH:(kd + 1) * TCH],
                                            tmpv[:, 0:1024], psh[1][:], OP.subtract).then_inc(s_v.h, 1)

                        b.emit("vector", xhatk)
                        s_v.n += 2
                    xhat_v = s_v.n

                    comb_v = {}
                    comb2_v = {}
                    ctr_v = {}
                    rt_t = {}
                    xsc_v_cur = 0
                    cb_v = 0
                    for e in range(NE):
                        eg = li * NE + e
                        col = l * NE + e
                        if e == 0:
                            b.wait("vector", s_t, hlp["sx_last_t"])
                        else:
                            b.wait("vector", s_t, hmm_last_t)

                        def xev(E, col=col):
                            last = None
                            for kd in range(8):
                                last = E.tensor_scalar(xb[:, kd * TCH:(kd + 1) * TCH],
                                                       xhat[:, kd * TCH:(kd + 1) * TCH],
                                                       gcs[:, col * 8 + kd:col * 8 + kd + 1],
                                                       bcs[:, col * 8 + kd:col * 8 + kd + 1],
                                                       OP.mult, OP.add)
                            last.then_inc(s_v.h, 1)

                        b.emit("vector", xev)
                        s_v.n += 1
                        xe_v = s_v.n

                        for fb in range(16):
                            b.wait("tensor", s_gd, wblk_n[eg * 32 + fb])
                            slot = fb % 2
                            for mf in range(2):
                                gmf = fb * 2 + mf
                                gidx = eg * 32 + gmf
                                ph = psh[gidx % 2]
                                if gidx >= 2:
                                    b.wait("tensor", s_sc, h_evac.get(gidx - 2, 0))
                                if fb == 0 and mf == 0:
                                    if e == 0:
                                        b.wait("tensor", s_v, xhat_v)
                                    b.wait("tensor", s_v, xe_v)

                                def hmm(E, slot=slot, mf=mf, ph=ph):
                                    last = None
                                    for kd in range(8):
                                        lhs = w1s[:, slot * 2048 + kd * 256 + mf * 128:
                                                  slot * 2048 + kd * 256 + (mf + 1) * 128]
                                        for tsc in range(2):
                                            last = E.matmul(
                                                ph[:, tsc * 512:(tsc + 1) * 512], lhs,
                                                xb[:, kd * TCH + tsc * 512: kd * TCH + (tsc + 1) * 512],
                                                start=(kd == 0), stop=(kd == 7))
                                    last.then_inc(s_t.h, 1)

                                b.emit("tensor", hmm)
                                s_t.n += 1
                                b.wait("scalar", s_t, s_t.n)
                                if eg == 0 and gidx == 0:
                                    b.wait("scalar", s_gd, static_ffn)

                                def geluv(E, gmf=gmf, ph=ph, col=col):
                                    E.activation(hten[:, gmf * TCH:(gmf + 1) * TCH], ph[:],
                                                 AF.Gelu,
                                                 bias=b1s[:, col * 32 + gmf:col * 32 + gmf + 1]
                                                 ).then_inc(s_sc.h, 1)

                                b.emit("scalar", geluv)
                                s_sc.n += 1
                                h_evac[gidx] = s_sc.n
                            blk_done[eg * 32 + fb] = s_t.n

                            if e == 0 and fb < 8:
                                jt = fb
                                if jt == 0:
                                    b.wait("tensor", s_v, hlp["rowev_v"])
                                    b.wait("tensor", s_v, psm_free_v)
                                    if l == 0:
                                        b.wait("tensor", s_sd, xc_done)
                                else:
                                    b.wait("tensor", s_v, comb_v[jt - 1])

                                def rtmm(E, l=l, jt=jt):
                                    last = None
                                    for kd in range(8):
                                        last = E.matmul(psm[:, 8:12],
                                                        x_c[:, kd * TCH + jt * 128: kd * TCH + (jt + 1) * 128],
                                                        rws[:, (l * 8 + kd) * 4:(l * 8 + kd + 1) * 4],
                                                        start=(kd == 0), stop=(kd == 7))
                                    last.then_inc(s_t.h, 1)

                                b.emit("tensor", rtmm)
                                s_t.n += 1
                                rt_t[jt] = s_t.n
                                b.wait("vector", s_t, rt_t[jt])

                                def combv(E, l=l, base=s_v.n):
                                    E.tensor_tensor(lgt[:, 0:4], psm[:, 8:12],
                                                    rbb_all[:, l * 4:(l + 1) * 4], OP.add).then_inc(s_v.h, 1)
                                    E.wait_ge(s_v.h, base + 1)
                                    E.reduce_max(lgt[:, 4:5], lgt[:, 0:4], axis=AX.X, negate=True).then_inc(s_v.h, 1)

                                b.emit("vector", combv)
                                s_v.n += 2
                                comb_v[jt] = s_v.n
                                b.wait("scalar", s_v, comb_v[jt])

                                def combe(E):
                                    E.activation(lgt[:, 8:12], lgt[:, 0:4], AF.Exp, bias=lgt[:, 4:5]).then_inc(s_sc.h, 1)

                                b.emit("scalar", combe)
                                s_sc.n += 1
                                b.wait("vector", s_sc, s_sc.n)

                                def combv2(E, jt=jt, base=s_v.n):
                                    el = lgt[:, 8:12]
                                    E.reduce_max(lgt[:, 5:6], el, axis=AX.X).then_inc(s_v.h, 1)
                                    E.wait_ge(s_v.h, base + 1)
                                    E.tensor_scalar(lgt[:, 12:16], el, lgt[:, 5:6], None, OP.is_ge, OP.bypass).then_inc(s_v.h, 1)
                                    E.wait_ge(s_v.h, base + 2)
                                    E.tensor_tensor(comb5[:, 4:8], el, lgt[:, 12:16], OP.mult).then_inc(s_v.h, 1)
                                    E.wait_ge(s_v.h, base + 3)
                                    E.tensor_tensor(comb5[:, 4:8], el, comb5[:, 4:8], OP.subtract).then_inc(s_v.h, 1)
                                    E.wait_ge(s_v.h, base + 4)
                                    E.reduce_max(lgt[:, 6:7], comb5[:, 4:8], axis=AX.X).then_inc(s_v.h, 1)
                                    E.wait_ge(s_v.h, base + 5)
                                    E.tensor_scalar(comb5[:, 4:8], el, lgt[:, 6:7], None, OP.is_ge, OP.bypass)
                                    E.tensor_scalar(lgt[:, 7:8], lgt[:, 6:7], 1.0, None, OP.add, OP.bypass).then_inc(s_v.h, 1)
                                    E.wait_ge(s_v.h, base + 6)
                                    E.reciprocal(lgt[:, 7:8], lgt[:, 7:8]).then_inc(s_v.h, 1)
                                    E.wait_ge(s_v.h, base + 7)
                                    E.scalar_tensor_tensor(comb_all[:, jt * 4:(jt + 1) * 4], el,
                                                           lgt[:, 7:8], comb5[:, 4:8],
                                                           OP.mult, OP.mult).then_inc(s_v.h, 1)

                                b.emit("vector", combv2)
                                s_v.n += 8
                                comb2_v[jt] = s_v.n

                            if e == 0 and 2 <= fb < 10:
                                jt = fb - 2
                                b.wait("tensor", s_v, comb2_v[jt])

                                def ctr(E, jt=jt):
                                    E.transpose(psm[0:4, 512:640], comb_all[:, jt * 4:(jt + 1) * 4],
                                                idn_s[:]).then_inc(s_t.h, 1)

                                b.emit("tensor", ctr)
                                s_t.n += 1
                                b.wait("vector", s_t, s_t.n)
                                if jt == 0:
                                    b.wait("vector", s_t, cb_t_last)

                                def ctrev(E, jt=jt):
                                    E.tensor_copy(c4[0:4, jt * 128:(jt + 1) * 128],
                                                  psm[0:4, 512:640]).then_inc(s_v.h, 1)

                                b.emit("vector", ctrev)
                                s_v.n += 1
                                ctr_v[jt] = s_v.n

                            if e == 0 and fb == 10:
                                b.wait("tensor", s_v, ctr_v[7])
                                b.wait("tensor", s_v, comb_v[7])

                                def csbmm(E):
                                    E.matmul(psm[:, 0:512], oh4s[:, 512:640], c4[0:4, 0:512],
                                             start=True, stop=True)
                                    E.matmul(psm[:, 512:1024], oh4s[:, 512:640], c4[0:4, 512:1024],
                                             start=True, stop=True).then_inc(s_t.h, 1)

                                b.emit("tensor", csbmm)
                                s_t.n += 1
                                csb_t = s_t.n
                                b.wait("vector", s_t, csb_t)
                                for kd in range(8):
                                    def xscale(E, kd=kd):
                                        E.tensor_tensor(x_c[:, kd * TCH:(kd + 1) * TCH],
                                                        x_c[:, kd * TCH:(kd + 1) * TCH],
                                                        psm[:, 0:1024], OP.mult).then_inc(s_v.h, 1)

                                    b.emit("vector", xscale)
                                    s_v.n += 1
                                xsc_v_cur = s_v.n

                            if fb == 12:
                                if e == 0:
                                    b.wait("tensor", s_v, xsc_v_cur)
                                else:
                                    b.wait("tensor", s_v, cb_ev_last)

                                def cbmm(E, e=e):
                                    E.matmul(psm[:, 0:512], oh4s[:, e * 128:(e + 1) * 128],
                                             c4[0:4, 0:512], start=True, stop=True)
                                    E.matmul(psm[:, 512:1024], oh4s[:, e * 128:(e + 1) * 128],
                                             c4[0:4, 512:1024], start=True, stop=True).then_inc(s_t.h, 1)

                                b.emit("tensor", cbmm)
                                s_t.n += 1
                                cb_t_last = s_t.n
                                b.wait("vector", s_t, cb_t_last)

                                def cbev(E):
                                    E.tensor_copy(cb_b[:], psm[:, 0:1024]).then_inc(s_v.h, 1)

                                b.emit("vector", cbev)
                                s_v.n += 1
                                cb_v = s_v.n
                                cb_ev_last = cb_v
                        hmm_last_t = s_t.n

                        is_last_e = (e == NE - 1)
                        nxt_layer = is_last_e and l < NL - 1
                        tail_here = is_last_e and l == NL - 1
                        for md in range(8):
                            if md == 0:
                                b.wait("tensor", s_v, cb_v)
                            for hf in range(2):
                                if md == 0:
                                    b.wait("tensor", s_sc,
                                           h_evac[eg * 32 + (15 if hf == 0 else 31)])
                                blkidx = eg * 32 + 16 + md * 2 + hf
                                b.wait("tensor", s_gd, wblk_n[blkidx])
                                slot = (md * 2 + hf) % 2
                                for tsc in range(2):
                                    gy = eg * 16 + md * 2 + tsc
                                    py = psy[gy % 2]
                                    if hf == 0 and gy >= 2:
                                        b.wait("tensor", s_v, y_evac.get(gy - 2, 0))

                                    def ymm(E, slot=slot, py=py, tsc=tsc, hf=hf):
                                        last = None
                                        for k in range(16):
                                            lhs = w2s[:, slot * 2048 + k * 128:slot * 2048 + (k + 1) * 128]
                                            last = E.matmul(
                                                py[:], lhs,
                                                hten[:, (hf * 16 + k) * TCH + tsc * 512:
                                                     (hf * 16 + k) * TCH + (tsc + 1) * 512],
                                                start=(hf == 0 and k == 0), stop=(hf == 1 and k == 15))
                                        last.then_inc(s_t.h, 1)

                                    b.emit("tensor", ymm)
                                    s_t.n += 1
                                    if hf == 1:
                                        b.wait("vector", s_t, s_t.n)

                                        def yev(E, md=md, tsc=tsc, py=py, gy=gy, col=col, base=s_v.n):
                                            tp = tmpv[:, (gy % 2) * 512:(gy % 2 + 1) * 512]
                                            E.scalar_tensor_tensor(tp, py[:],
                                                                   b2s[:, col * 8 + md:col * 8 + md + 1],
                                                                   cb_b[:, tsc * 512:(tsc + 1) * 512],
                                                                   OP.add, OP.mult).then_inc(s_v.h, 1)
                                            E.wait_ge(s_v.h, base + 1)
                                            sl = x_c[:, md * TCH + tsc * 512: md * TCH + (tsc + 1) * 512]
                                            E.tensor_tensor(sl, sl, tp, OP.add).then_inc(s_v.h, 1)

                                        b.emit("vector", yev)
                                        s_v.n += 2
                                        y_evac[gy] = s_v.n - 1

                                        if nxt_layer and tsc == 1:
                                            kd = md
                                            if kd == 0:
                                                b.wait("vector", s_t, hmm_last_t)

                                            def xbsq(E, kd=kd, base=s_v.n):
                                                E.tensor_copy(xb[:, kd * TCH:(kd + 1) * TCH],
                                                              x_c[:, kd * TCH:(kd + 1) * TCH]).then_inc(s_v.h, 1)
                                                E.wait_ge(s_v.h, base + 1)
                                                E.tensor_tensor(sqtmp[:, kd * TCH:(kd + 1) * TCH],
                                                                xb[:, kd * TCH:(kd + 1) * TCH],
                                                                xb[:, kd * TCH:(kd + 1) * TCH],
                                                                OP.mult).then_inc(s_v.h, 1)

                                            b.emit("vector", xbsq)
                                            s_v.n += 2
                                            sq_v[kd] = s_v.n
                                blk_done[blkidx] = s_t.n
                            if nxt_layer and md >= 1:
                                emit_sxk(md - 1, False)
                            if tail_here:
                                kd = md
                                if kd >= 1:
                                    b.wait("sync", s_v, prod_v[kd - 1])
                                s_dma[kd] = dma("sync", sti[:, 0:TCH],
                                                bass.AP(s_s, kd * T + c * TCH, [[8 * T, 128], [1, TCH]]))
                                b.wait("scalar", s_v, y_evac[eg * 16 + kd * 2 + 1] + 1)
                                if kd >= 1:
                                    b.wait("scalar", s_v, prod_v[kd - 1])

                                def siluk(E, kd=kd):
                                    E.activation(gbuf[:, 0:TCH],
                                                 x_c[:, kd * TCH:(kd + 1) * TCH],
                                                 AF.Silu).then_inc(s_sc.h, 1)

                                b.emit("scalar", siluk)
                                s_sc.n += 1
                                b.wait("vector", s_sc, s_sc.n)
                                b.wait("vector", s_sd, s_dma[kd])

                                def prodk(E, kd=kd):
                                    E.tensor_tensor(x_c[:, kd * TCH:(kd + 1) * TCH],
                                                    sti[:, 0:TCH],
                                                    gbuf[:, 0:TCH],
                                                    OP.mult).then_inc(s_v.h, 1)

                                b.emit("vector", prodk)
                                s_v.n += 1
                                prod_v[kd] = s_v.n
                        if nxt_layer:
                            emit_sxk(7, True)
                            emit_rows()
                            pend_sums = True
                        xb_free_t = hmm_last_t
                        emit_weight_dmas(c, l, e)
                    psm_free_v = cb_ev_last

                if c < NCH - 1:
                    emit_l0_rows(c + 1, bc_t, hlp["wrow_v"])
                    pend_sums = True

                out_v = {}
                xi_dma = {}
                for jt in range(8):
                    gjt = c * 8 + jt
                    if jt >= 1 or c > 0:
                        b.wait("sync", s_sd, out_dma_prev)
                    xi_dma[jt] = dma("sync", outsb[:],
                                     x_in.ap()[gjt * 128:(gjt + 1) * 128, :])
                    for half in range(2):
                        b.wait("tensor", s_v, prod_v[7])
                        if jt >= 1 or half >= 1:
                            b.wait("tensor", s_v, prtr_ev)
                        if jt == 0 and half == 0:
                            b.wait("tensor", s_sc, h_evac[((c * NL + NL - 1) * NE + NE - 1) * 32 + 31])

                        def prtr(E, jt=jt, half=half):
                            last = None
                            for k in range(4):
                                kd = half * 4 + k
                                last = E.transpose(psh[half][:, k * 128:(k + 1) * 128],
                                                   x_c[:, kd * TCH + jt * 128: kd * TCH + (jt + 1) * 128],
                                                   idn_s[:])
                            last.then_inc(s_t.h, 1)

                        b.emit("tensor", prtr)
                        s_t.n += 1
                        b.wait("vector", s_t, s_t.n)
                        b.wait("vector", s_sd, xi_dma[jt])

                        def prtrev(E, half=half):
                            last = None
                            for k in range(4):
                                kd = half * 4 + k
                                last = E.tensor_tensor(
                                    outsb[:, kd * 128:(kd + 1) * 128],
                                    outsb[:, kd * 128:(kd + 1) * 128],
                                    psh[half][:, k * 128:(k + 1) * 128], OP.add)
                            last.then_inc(s_v.h, 1)

                        b.emit("vector", prtrev)
                        s_v.n += 1
                        prtr_ev = s_v.n
                    prtr_jt_t = s_t.n
                    out_v[jt] = s_v.n
                    b.wait("sync", s_v, out_v[jt])
                    od = dma("sync", out.ap()[gjt * 128:(gjt + 1) * 128, :], outsb[:])
                    out_dma_prev = od
                    if c < NCH - 1:
                        b.wait("scalar", s_t, prtr_jt_t)
                        xc_done = dma(
                            "scalar",
                            x_c.ap().rearrange("p (k t) -> p k t", k=8)[:, :, jt * 128:(jt + 1) * 128],
                            bass.AP(xn_s, (c + 1) * TCH + jt * 128,
                                    [[8 * T, 128], [T, 8], [1, 128]]))
                prtr_last_v = prtr_ev
                combine_done_v[c] = prtr_ev

            b.wait("sync", s_sd, s_sd.n)

        blk = es0.enter_context(nc.Block())
        b.run(blk)
    return nc


def kernel(**inputs):
    f32 = np.float32
    g = lambda k: np.asarray(inputs[k], dtype=f32)
    x = g("L0_attn_x")
    wq_d = g("L0_attn_wq_d")
    bq_d = g("L0_attn_bq_d")
    w_qk = g("L0_attn_w_qk")
    b_qk = g("L0_attn_b_qk")
    wkv_d = g("L0_attn_wkv_d")
    bkv_d = g("L0_attn_bkv_d")
    wv_u = g("L0_attn_wv_u")
    bv_u = g("L0_attn_bv_u")
    w_proj = g("L0_attn_w_proj")
    b_proj = g("L0_attn_b_proj")
    rw = g("L0_ffn_router_w")
    rb = g("L0_ffn_router_b")
    ln_g = g("L0_ffn_ln_g")
    ln_b = g("L0_ffn_ln_b")
    w1 = g("L0_ffn_w1")
    b1 = g("L0_ffn_b1")
    w2 = g("L0_ffn_w2")
    b2 = g("L0_ffn_b2")

    C = np.ascontiguousarray

    def _mk_oh4():
        oh = np.zeros((4, 640), dtype=f32)
        for e in range(4):
            oh[e, e * 128:(e + 1) * 128] = 1.0
        oh[:, 512:640] = 1.0
        return oh

    feed = {
        "idn": np.eye(128, dtype=f32),
        "wqd": C(wq_d.T.reshape(8, 128, 4, 128).transpose(1, 0, 2, 3).reshape(128, 4096)),
        "wqk": C(w_qk.T.reshape(4, 128, 16, 64).transpose(1, 0, 2, 3).reshape(128, 4096)),
        "wkv": C(wkv_d.T.reshape(8, 128, 64).transpose(1, 0, 2).reshape(128, 512)),
        "wvu": C(wv_u.T),
        "wpj": C(w_proj.T.reshape(8, 128, 8, 128).transpose(1, 0, 2, 3).reshape(128, 8192)),
        "bqd": C(bq_d.reshape(4, 128).T),
        "bqk": C(b_qk.reshape(16, 64).T),
        "bkv": C(bkv_d.reshape(64, 1)),
        "bvu": C(bv_u.reshape(8, 128).T),
        "bpj": C(b_proj.reshape(8, 128).T),
        "w1l": C(w1.reshape(NL, NE, 8, 128, 16, 256).transpose(0, 1, 4, 3, 2, 5)
                 .reshape(NL, NE, 16, 128, 2048)),
        "w2l": C(w2.reshape(NL, NE, 2, 16, 128, 8, 128).transpose(0, 1, 5, 2, 4, 3, 6)
                 .reshape(NL, NE, 16, 128, 2048)),
        "b1c": C(b1.reshape(NL, NE, 32, 128).transpose(3, 0, 1, 2).reshape(128, NL * NE * 32)),
        "b2c": C(b2.reshape(NL, NE, 8, 128).transpose(3, 0, 1, 2).reshape(128, NL * NE * 8)),
        "rwl": C(rw.reshape(NL, 8, 128, 4).transpose(2, 0, 1, 3).reshape(128, NL * 8 * 4)),
        "rbr": C(rb.reshape(1, NL * 4)),
        "gcl": C(ln_g.reshape(NL, NE, 8, 128).transpose(3, 0, 1, 2).reshape(128, NL * NE * 8)),
        "oh4": _mk_oh4(),
        "bcl": C(ln_b.reshape(NL, NE, 8, 128).transpose(3, 0, 1, 2).reshape(128, NL * NE * 8)),
    }

    nc = bass.Bass("TRN2", target_bir_lowering=False)
    build(nc)
    in_maps = []
    for core in range(8):
        m = dict(feed)
        m["x_in"] = C(x[core])
        in_maps.append(m)
    res = run_bass_kernel_spmd(nc, in_maps, core_ids=list(range(8)))
    return np.stack([np.asarray(r["out"]) for r in res.results], axis=0)

